# version 13
# baseline (speedup 1.0000x reference)
import contextlib
import numpy as np
import ml_dtypes
import concourse.bass as bass
import concourse.mybir as mybir
from concourse.bass_utils import run_bass_kernel_spmd

F32 = mybir.dt.float32
BF16 = mybir.dt.bfloat16
FP8 = mybir.dt.float8e4
AF = mybir.ActivationFunctionType
ALU = mybir.AluOpType
AXX = mybir.AxisListType.X

D = 2048
E_ALL = 8384
DI = 4096
NH = 64
NKT = 16
TB = 256
NB = 8
CH = 128
NCH = TB // CH
NXT = 32
NCT = 33
NET = 34
STW = 259
EPS = 1e-5

DEV_NB = None
DEV_SKIP_RECOMPUTE = False
DEV_STOP = None
DEBUG_TAP = None

_BUILD_CACHE = {}


def build_nc(dev_nb=None, skip_recompute=False, debug_tap=None, dev_stop=None):
    nb = dev_nb or NB
    nc = bass.Bass("TRN2", target_bir_lowering=False)

    dx = nc.dram_tensor("x_own", [2048, D], F32, kind="ExternalInput").ap()
    dxp = nc.dram_tensor("x_prev", [2048, D], F32, kind="ExternalInput").ap()
    dflag = nc.dram_tensor("flag", [64, 1], F32, kind="ExternalInput").ap()
    dWT = nc.dram_tensor("WTb", [128, 66 * 2048 - 1024], F32, kind="ExternalInput").ap()
    dWoT = nc.dram_tensor("WoTb", [128, 32 * 2048], F32, kind="ExternalInput").ap()
    dWg0 = nc.dram_tensor("Wg0", [D, 8192], F32, kind="ExternalInput").ap()
    dWg1 = nc.dram_tensor("Wg1", [D, 8192], F32, kind="ExternalInput").ap()
    dWl = nc.dram_tensor("Wl", [8192, D], F32, kind="ExternalInput").ap()
    dcw = nc.dram_tensor("convw", [128, NCT, 4], F32, kind="ExternalInput").ap()
    dcb = nc.dram_tensor("convb", [128, NCT], F32, kind="ExternalInput").ap()
    drms = nc.dram_tensor("rms", [128, NKT], F32, kind="ExternalInput").ap()
    dnorm = nc.dram_tensor("normw", [128, NXT], F32, kind="ExternalInput").ap()
    ddtb = nc.dram_tensor("dtb", [64, 1], F32, kind="ExternalInput").ap()
    dalog = nc.dram_tensor("alog", [64, 1], F32, kind="ExternalInput").ap()
    dDrep = nc.dram_tensor("Drep", [128, 64], F32, kind="ExternalInput").ap()
    dI32 = nc.dram_tensor("ident32", [128, 128], F32, kind="ExternalInput").ap()
    dIbf = nc.dram_tensor("identbf", [128, 128], BF16, kind="ExternalInput").ap()
    dones = nc.dram_tensor("onesc", [128, 128], F32, kind="ExternalInput").ap()
    dsuneg = nc.dram_tensor("sunegt4", [128, 512], F32, kind="ExternalInput").ap()
    dlas = nc.dram_tensor("lascr", [64, 256], F32, kind="Internal").ap()
    depsone = nc.dram_tensor("epsone", [128, 2], F32, kind="ExternalInput").ap()
    dout = nc.dram_tensor("out", [2048, D], F32, kind="ExternalOutput").ap()
    dg0q = nc.dram_tensor("g0q", [128, 16 * 8192], FP8, kind="Internal").ap()
    dg1q = nc.dram_tensor("g1q", [128, 16 * 8192], FP8, kind="Internal").ap()
    dwlq = nc.dram_tensor("wlq", [128, 64 * 2048], FP8, kind="Internal").ap()
    ddbg = None
    dbg_cols = 40960
    if debug_tap is not None:
        ddbg = nc.dram_tensor("dbg", [128, dbg_cols], F32, kind="ExternalOutput").ap()

    streams = {k: [] for k in ("tensor", "vector", "scalar", "gpsimd", "sync")}

    class Sem:
        __slots__ = ("h", "n")

        def __init__(self, h):
            self.h = h
            self.n = 0

    def em(eng, fn):
        streams[eng].append(fn)

    def wait(eng, sem, val):
        if val and val > 0:
            em(eng, lambda e, s=sem.h, v=val: e.wait_ge(s, v))

    _SERIAL_ENGS = ("vector", "scalar", "gpsimd")

    def run(eng, builder, sem=None):
        if eng in _SERIAL_ENGS:
            own = ENG_SEM[eng]
            if sem is None:
                sem = own
            if own.n:
                em(eng, lambda e, s=own.h, v=own.n: e.wait_ge(s, v))
        if sem is None:
            em(eng, builder)
            return None
        sem.n += 1
        em(eng, lambda e, b=builder, s=sem.h: b(e).then_inc(s, 1))
        return sem.n

    def dma(qeng, out_ap, in_ap, sem):
        sem.n += 16
        em(qeng, lambda e, o=out_ap, i=in_ap, s=sem.h: e.dma_start(out=o, in_=i).then_inc(s, 16))
        return sem.n

    ENG_SEM = {}

    class Bank:
        __slots__ = ("ap", "cons", "wrote")

        def __init__(self, ap):
            self.ap = ap
            self.cons = []
            self.wrote = None

        def begin_write(self, eng="tensor"):
            for s, v in self.cons:
                wait(eng, s, v)
            self.cons = []

        def wrote_group(self, val):
            self.wrote = (ENG_SEM["tensor"], val)

        def read(self, eng):
            if self.wrote is not None:
                wait(eng, self.wrote[0], self.wrote[1])

        def consumed(self, sem, val):
            self.cons.append((sem, val))

    with contextlib.ExitStack() as ctx:
        def sb(name, shape, dt=F32):
            return ctx.enter_context(nc.sbuf_tensor(name, shape, dt))

        ar_stage = sb("stage", [128, NET * STW])
        ar_ut = sb("ut", [128, NKT * TB])
        ar_y = sb("ybuf", [128, 2 * DI])
        ar_x1 = sb("x1b", [128, 2 * D])
        ar_u = sb("ubuf", [128, D])
        wA = sb("wbufA", [128, 2048])
        wB = sb("wbufB", [128, 2048])
        s_a = sb("s_a", [64, DI])
        s_b = sb("s_b", [64, DI])
        xt0 = sb("xt0", [128, 512])
        xt1 = sb("xt1", [128, 512])
        acc0 = sb("acc0", [128, TB])
        acc1 = sb("acc1", [128, TB])
        accg = sb("accg", [128, TB])
        acc2 = sb("acc2", [128, TB])
        accg2 = sb("accg2", [128, TB])
        yscr0 = sb("yscr0", [128, 64])
        yscr1 = sb("yscr1", [128, 64])
        ycr0 = sb("ycr0", [128, 64])
        ycr1 = sb("ycr1", [128, 64])
        bct = sb("bct", [128, 128])
        g_sb = sb("g_sb", [128, 128])
        c0sb = sb("c0sb", [64, 128])
        e_a = sb("e_a", [128, 128])
        e_b = sb("e_b", [128, 128])
        p4A = sb("p4A", [128, 2048])
        p4B = sb("p4B", [128, 2048])
        p4m_a = sb("p4m_a", [128, 512])
        p4m_b = sb("p4m_b", [128, 512])
        m_a = sb("m_a", [128, 128])
        m_b = sb("m_b", [128, 128])
        bwsb = sb("bwsb", [128, 64])
        bwsb2 = sb("bwsb2", [128, 64])
        dtraw = sb("dtraw", [64, TB])
        dtval = sb("dtval", [64, TB])
        lndt = sb("lndt", [64, TB])
        dta = sb("dta", [64, TB])
        lat = sb("lat", [64, TB])
        laT = sb("laT", [128, NCH * 64])
        lndtT = sb("lndtT", [128, NCH * 64])
        biascol = sb("biascol", [128, NCH * 64])
        elatok = sb("elatok", [128, NCH * 64])
        elac = sb("elac", [64, NCH * 64])
        wcols = sb("wcols", [128, NCH * 64])
        lcbsb = sb("lcbsb", [64, 128])
        colsb = sb("colsb", [128, 16])
        gcol = sb("gcol", [128, 4])
        ssqcol = sb("ssqcol", [128, 2 * NXT + 2])
        halo3 = sb("halo3", [128, NCT * 3])
        gatesb = sb("gatesb", [128, 512], BF16)
        upsb = sb("upsb", [128, 512], BF16)
        prodsb = sb("prodsb", [128, 512], BF16)
        cwsb = sb("cwsb", [128, NCT * 4])
        cbsb = sb("cbsb", [128, NCT])
        rmssb = sb("rmssb", [128, NKT])
        normsb = sb("normsb", [128, NXT])
        dtbsb = sb("dtbsb", [64, 1])
        alogsb = sb("alogsb", [64, 1])
        acolsb = sb("acolsb", [64, 1])
        flagsb = sb("flagsb", [64, 1])
        drepsb = sb("drepsb", [128, 64])
        i32sb = sb("i32sb", [128, 128])
        ibfsb = sb("ibfsb", [128, 128], BF16)
        onessb = sb("onessb", [128, 128])
        sunegsb = sb("sunegsb", [128, 512])
        epsonesb = sb("epsonesb", [128, 2])

        st8 = ar_stage.ap().bitcast(FP8)
        aqT = lambda: st8[:, 0:64 * TB]
        xqT = lambda: st8[:, 64 * TB:64 * TB + NKT * TB]
        ln2scr = ar_stage.ap()[:, 5120:5120 + D]
        ynt_ap = ar_stage.ap()
        lnscr = ar_x1.ap()[:, 0:D]
        xin1 = ar_y.ap()[:, 3 * D:4 * D]
        yq = ar_y.ap()
        yq8 = ar_y.ap().bitcast(FP8)

        PS = [ctx.enter_context(nc.psum_tensor(f"ps{i}", [128, 512], F32)) for i in range(8)]
        banks = [Bank(PS[i].ap()) for i in range(8)]

        sem_names = ("LD", "SP", "TE", "VE", "AC", "GP",
                     "sX", "sX2A", "sX2B", "sWA", "sWB",
                     "sQ0", "sQ1", "sPO0", "sPO1", "sOUT", "sDBG", "sLA",
                     "sWC", "sWD")
        SEMH = {k: ctx.enter_context(nc.semaphore(k)) for k in sem_names}
        (LD, SP, TE, VE, AC, GP, sX, sX2A, sX2B, sWA, sWB,
         sQ0, sQ1, sPO0, sPO1, sOUT, sDBG, sLA,
         sWC, sWD) = (Sem(SEMH[k]) for k in sem_names)
        ENG_SEM.update(tensor=TE, vector=VE, scalar=AC, gpsimd=GP, sync=LD)

        taps = {}

        def tap(name, ap):
            taps[name] = ap

        def tpose(eng_unused, bank_slice, in_ap):
            p = in_ap.shape[0]
            return lambda e, b=bank_slice, i=in_ap, p=p: e.transpose(b, i, i32sb[0:p, 0:p])

        for dst, src in ((cwsb, dcw.rearrange("p c k -> p (c k)")), (cbsb, dcb[:, :]),
                         (rmssb, drms[:, :]), (normsb, dnorm[:, :]), (dtbsb, ddtb[:, :]),
                         (alogsb, dalog[:, :]), (flagsb, dflag[:, :]), (drepsb, dDrep[:, :]),
                         (i32sb, dI32[:, :]), (ibfsb, dIbf[:, :]), (onessb, dones[:, :]),
                         (sunegsb, dsuneg[:, :]), (epsonesb, depsone[:, :])):
            if LD.n:
                wait("sync", LD, LD.n)
            v0 = dma("sync", dst[:, :], src, LD)
        cEPS = epsonesb[:, 0:1]
        cONE = epsonesb[:, 1:2]
        wait("scalar", LD, v0)
        run("scalar", lambda e: e.activation(acolsb[:, :], alogsb[:, :], AF.Exp), AC)
        wait("vector", AC, AC.n)
        run("vector", lambda e: e.tensor_scalar_mul(acolsb[:, :], acolsb[:, :], -1.0), VE)
        run("vector", lambda e: e.memset(s_a[:, :], 0.0), VE)
        run("vector", lambda e: e.memset(halo3[:, :], 0.0), VE)
        run("vector", lambda e: e.memset(colsb[:, :], 0.0), VE)
        run("vector", lambda e: e.memset(ssqcol[:, :], 0.0), VE)
        run("vector", lambda e: e.memset(gcol[:, :], 0.0), VE)
        for t_ in (ar_stage, ar_ut, ar_y, ar_x1, ar_u, wA, wB, s_b):
            run("vector", lambda e, t_=t_: e.memset(t_[:, :], 0.0), VE)
        v_init = VE.n

        q_jobs = []
        dg0qv = dg0q.rearrange("p (hb w) -> p hb w", w=8192)
        dg1qv = dg1q.rearrange("p (hb w) -> p hb w", w=8192)
        for k in range(NKT):
            for wsrc, wdstv in ((dWg0, dg0qv), (dWg1, dg1qv)):
                q_jobs.append([(wsrc[k * 128:(k + 1) * 128, j * 2048:(j + 1) * 2048],
                                wdstv[:, 4 * j: 4 * j + 4, k * 512:(k + 1) * 512])
                               for j in range(4)])
        wlr = dWl.rearrange("(a p) d -> p a d", p=128)
        for k in range(16):
            q_jobs.append([(wlr[:, k * 4 + j, :],
                            dwlq[:, (k * 4 + j) * 2048:(k * 4 + j + 1) * 2048]
                            .rearrange("p (u h) -> p u h", u=4))
                           for j in range(4)])
        qstate = {"i": 0, "use": {0: (0, 0), 1: (0, 0)}}

        def emit_q_jobs(count):
            for _ in range(count):
                i = qstate["i"]
                if i >= len(q_jobs):
                    return
                qstate["i"] += 1
                if i == 0:
                    wait("sync", VE, v_init)
                for j, (src, dst) in enumerate(q_jobs[i]):
                    par = (4 * i + j) % 2
                    sQ = sQ0 if par == 0 else sQ1
                    sPO = sPO0 if par == 0 else sPO1
                    f32v = yq[:, par * 2048: par * 2048 + 2048]
                    f8v = yq8[:, 16384 + par * 2048: 16384 + par * 2048 + 2048]
                    pg, pspv = qstate["use"][par]
                    wait("sync", VE, pg)
                    lv = dma("sync", f32v, src, sQ)
                    wait("vector", sQ, lv)
                    wait("vector", sPO, pspv)
                    gv = run("vector", lambda e, o=f8v, s=f32v: e.tensor_copy(o, s), VE)
                    wait("scalar", VE, gv)
                    sv = dma("scalar", dst,
                             f8v.rearrange("p (u h) -> p u h", u=4), sPO)
                    qstate["use"][par] = (gv, sv)
            return

        def q_done_val():
            return (sPO0.n, sPO1.n) if qstate["i"] >= len(q_jobs) else None

        st = {"sparity": 0, "yfree": 0, "zfree": 0, "stagefree_v": 0,
              "stagefree_g": 0, "stagefree_a": 0, "stagefree_t": 0,
              "uT_cons": 0, "xin_free_v": v_init, "xin_free_t": 0,
              "wfree_v": v_init}

        def emit_ln_uT(xsrc, blk):
            for tt in range(2):
                xin = xin1
                wait("sync", VE, st["xin_free_v"])
                wait("sync", AC, st.get("xin_free_a", 0))
                wait("sync", GP, st.get("y_done", 0))
                lv = dma("sync", xin, xsrc[blk * TB + tt * 128: blk * TB + (tt + 1) * 128, :], sX)
                wait("vector", sX, lv)
                mcol = colsb[:, 8 * tt + 0: 8 * tt + 1]
                vcol = colsb[:, 8 * tt + 1: 8 * tt + 2]
                rstd = colsb[:, 8 * tt + 2: 8 * tt + 3]
                stok = colsb[:, 8 * tt + 3: 8 * tt + 4]
                tmp = colsb[:, 8 * tt + 4: 8 * tt + 5]
                wait("vector", AC, st.get("colsb_ac", 0))
                rv = run("vector", lambda e, x=xin, o=mcol: e.reduce_sum(o, x, axis=AXX), VE)
                wait("vector", VE, rv)
                mv0 = run("vector", lambda e, o=mcol: e.tensor_scalar_mul(o, o, 1.0 / D), VE)
                wait("vector", VE, mv0)
                wait("vector", TE, st["uT_cons"])
                run("vector", lambda e, x=xin, m=mcol: e.tensor_scalar(
                    out=ar_u[:, :], in0=x, scalar1=m, scalar2=None, op0=ALU.subtract), VE)
                wait("vector", sOUT, st.get("x1_free", 0))
                run("vector", lambda e, v=vcol: e.scalar_tensor_tensor(
                    out=lnscr, in0=ar_u[:, :], scalar=1.0, in1=ar_u[:, :],
                    op0=ALU.mult, op1=ALU.mult, accum_out=v), VE)
                wait("vector", VE, VE.n)
                vv = run("vector", lambda e, v=vcol: e.tensor_scalar_mul(v, v, 1.0 / D), VE)
                wait("scalar", VE, vv)
                run("scalar", lambda e, v=vcol, t=tmp: e.activation(t, v, AF.Ln, bias=cEPS), AC)
                av = run("scalar", lambda e, t=tmp, r=rstd: e.activation(r, t, AF.Exp, scale=-0.5), AC)
                wait("vector", AC, av)
                run("vector", lambda e, v=vcol, t=tmp: e.tensor_scalar(
                    out=t, in0=v, scalar1=float(EPS), scalar2=None, op0=ALU.add), VE)
                run("vector", lambda e, t=tmp: e.reciprocal(t, t), VE)
                vv = run("vector", lambda e, v=vcol, t=tmp: e.tensor_tensor(t, v, t, ALU.mult), VE)
                wait("scalar", VE, vv)
                run("scalar", lambda e, t=tmp: e.activation(t, t, AF.Ln, bias=cEPS), AC)
                av = run("scalar", lambda e, t=tmp: e.activation(t, t, AF.Exp, scale=-0.5), AC)
                wait("vector", AC, av)
                sv0 = run("vector", lambda e, r=rstd, t=tmp, s=stok: e.tensor_tensor(s, r, t, ALU.mult), VE)
                wait("vector", VE, sv0)
                uv = run("vector", lambda e, s=stok: e.tensor_scalar(
                    out=ar_u[:, :], in0=ar_u[:, :], scalar1=s, scalar2=None, op0=ALU.mult), VE)
                st["xin_free_v"] = uv
                wait("tensor", VE, uv)
                wait("vector", TE, st.get("ut_free", 0))
                for k in range(NKT):
                    bk = banks[6 + (k % 2)]
                    bk.begin_write()
                    tv = run("tensor", lambda e, k=k, b=bk: e.transpose(
                        b.ap[:, 0:128], ar_u[:, k * 128:(k + 1) * 128], i32sb[:, :]), TE)
                    bk.wrote_group(tv)
                    bk.read("scalar")
                    ev = run("scalar", lambda e, k=k, tt=tt, b=bk: e.activation(
                        ar_ut[:, k * TB + tt * 128: k * TB + tt * 128 + 128],
                        b.ap[:, 0:128], AF.Copy, bias=0.0,
                        scale=rmssb[:, k:k + 1]), AC)
                    bk.consumed(AC, ev)
                st["uT_cons"] = TE.n
            st["uT_ready"] = VE.n
            st["uT_ready_a"] = AC.n

        def emit_inproj_xbcdt(blk, part):
            if part == "head":
                wait("tensor", VE, st["uT_ready"])
                wait("tensor", AC, st.get("uT_ready_a", 0))
                wait("vector", AC, st["stagefree_a"])
                wait("vector", GP, st["stagefree_g"])
                wait("vector", TE, st["stagefree_t"])
                wait("vector", AC, st.get("dt_free", 0))
                wait("vector", GP, st.get("halo_w_g", 0))
                st["halo_v"] = run("vector", lambda e: e.tensor_copy(
                    ar_stage[:, 0:NCT * STW].rearrange("p (c w) -> p c w", c=NCT)[:, :, 0:3],
                    halo3[:, 0:NCT * 3].rearrange("p (c w) -> p c w", c=NCT)), VE)
                seq = [(0, 33)]
            else:
                seq = list(enumerate([32] + list(range(32)), start=1))
            for j, i in seq:
                et = 32 + i
                w = wA if j % 2 == 0 else wB
                ecols = 128 if et < 65 else 64
                qe = "sync" if j % 2 == 0 else "scalar"
                wait(qe, TE, st.get(f"wc{j % 2}", 0))
                wait(qe, VE, st["wfree_v"])
                sw = sWA if j % 2 == 0 else sWB
                lv = dma(qe, w[:, 0:NKT * ecols],
                         dWT[:, et * 2048: et * 2048 + NKT * ecols], sw)
                bk = banks[j % 4]
                pa = bk.ap[:, 0:256]
                wait("tensor", sw, lv)
                bk.begin_write()
                for k in range(NKT):
                    is_last = (k == NKT - 1)
                    sem = TE if is_last else None
                    run("tensor", lambda e, k=k, w=w, pa=pa, ec=ecols: e.matmul(
                        pa[0:ec, :] if ec < 128 else pa,
                        w[:, k * ec:(k + 1) * ec],
                        ar_ut[:, k * TB:(k + 1) * TB],
                        start=(k == 0), stop=(k == NKT - 1)), sem)
                tv = TE.n
                st[f"wc{j % 2}"] = tv
                bk.wrote_group(tv)
                bk.read("scalar")
                if et < 65:
                    ev = run("scalar", lambda e, i=i, pa=pa: e.copy(
                        ar_stage[:, i * STW + 3: i * STW + 3 + TB], pa), AC)
                    st[f"stev_{i}"] = ev
                else:
                    ev = run("scalar", lambda e, pa=pa: e.copy(
                        dtraw[:, :], pa[0:64, :]), AC)
                    st["dtraw_v"] = ev
                bk.consumed(AC, ev)
            if part == "rest":
                st["stage_ready"] = AC.n

        def emit_conv(blk):
            gpj = 0
            for j, i in enumerate([32] + list(range(32))):
                on_gp = (j % 2 == 1)
                eng = "gpsimd" if on_gp else "vector"
                sem = GP if on_gp else VE
                base = i * STW
                wait(eng, AC, st.get(f"stev_{i}", 0))
                wait(eng, VE, st["halo_v"])
                if on_gp:
                    aa, ag = (acc1, accg) if gpj % 2 == 0 else (acc2, accg2)
                    gpj += 1
                    wait(eng, AC, st.get(f"acc_cons_gp{gpj % 2}", 0))
                    run(eng, lambda e, base=base, i=i, aa=aa: e.tensor_tensor(
                        aa[:, :], ar_stage[:, base: base + TB],
                        cwsb[:, 4 * i: 4 * i + 1].broadcast_to([128, TB]),
                        ALU.mult), sem)
                    for kk in range(1, 4):
                        run(eng, lambda e, base=base, kk=kk, i=i, ag=ag: e.tensor_tensor(
                            ag[:, :], ar_stage[:, base + kk: base + kk + TB],
                            cwsb[:, 4 * i + kk: 4 * i + kk + 1].broadcast_to([128, TB]),
                            ALU.mult), sem)
                        run(eng, lambda e, aa=aa, ag=ag: e.tensor_tensor(
                            aa[:, :], aa[:, :], ag[:, :], ALU.add), sem)
                    accv = aa
                    acck = f"acc_cons_gp{gpj % 2}"
                else:
                    wait(eng, AC, st.get("acc_cons_ve", 0))
                    run(eng, lambda e, base=base, i=i: e.tensor_scalar(
                        out=acc0[:, :], in0=ar_stage[:, base: base + TB],
                        scalar1=cwsb[:, 4 * i: 4 * i + 1], scalar2=None, op0=ALU.mult), sem)
                    for kk in range(1, 4):
                        run(eng, lambda e, base=base, kk=kk, i=i: e.scalar_tensor_tensor(
                            out=acc0[:, :], in0=ar_stage[:, base + kk: base + kk + TB],
                            scalar=cwsb[:, 4 * i + kk: 4 * i + kk + 1], in1=acc0[:, :],
                            op0=ALU.mult, op1=ALU.add), sem)
                    accv = acc0
                    acck = "acc_cons_ve"
                cv = run(eng, lambda e, base=base, i=i: e.tensor_copy(
                    halo3[:, i * 3: i * 3 + 3], ar_stage[:, base + TB: base + TB + 3]), sem)
                wait("scalar", sem, cv)
                run("scalar", lambda e, base=base, a=accv, i=i: e.activation(
                    ar_stage[:, base + 3: base + 3 + TB], a[:, :], AF.Silu,
                    bias=cbsb[:, i:i + 1]), AC)
                st[acck] = AC.n
                st[f"convt_{i}"] = AC.n
            st["conv_done"] = AC.n
            st["halo_w_g"] = GP.n

        def emit_dt_head(blk):
            wait("scalar", AC, st["dtraw_v"])
            wait("scalar", TE, st.get("scan_te", 0))
            wait("vector", TE, st.get("scan_te", 0))
            wait("vector", AC, st.get("scan_ac", 0))
            run("scalar", lambda e: e.activation(dtval[:, :], dtraw[:, :], AF.Exp,
                                                 bias=dtbsb[:, :]), AC)
            run("scalar", lambda e: e.activation(dtval[:, :], dtval[:, :], AF.Ln, bias=cONE[0:64, :]), AC)
            av = run("scalar", lambda e: e.activation(lndt[:, :], dtval[:, :], AF.Ln), AC)
            wait("vector", AC, av)
            run("vector", lambda e: e.tensor_scalar(out=dta[:, :], in0=dtval[:, :],
                                                    scalar1=acolsb[:, :], scalar2=None,
                                                    op0=ALU.mult), VE)
            wait("vector", sLA, st.get("las_h", 0))
            for c in range(NCH):
                run("vector", lambda e, c=c: e.tensor_tensor_scan(
                    out=lat[:, c * CH:(c + 1) * CH], data0=onessb[0:64, 0:CH],
                    data1=dta[:, c * CH:(c + 1) * CH], initial=0.0,
                    op0=ALU.mult, op1=ALU.add), VE)
            lav = VE.n
            st["lav"] = lav

        def emit_dt_tail(blk, with_wcols):
            lav = st["lav"]
            wait("sync", VE, lav)
            wait("sync", sLA, st.get("sla_last", 0))
            st["sla_last"] = dma("sync", dlas[:, :], lat[:, :], sLA)
            st["las_h"] = st["sla_last"]
            wait("tensor", VE, lav)
            bk = banks[7]
            for c in range(NCH):
                for srcap, dst in ((lat[:, c * CH:(c + 1) * CH], laT),
                                   (lndt[:, c * CH:(c + 1) * CH], lndtT)):
                    bk.begin_write()
                    tv = run("tensor", lambda e, s=srcap, b=bk: e.transpose(
                        b.ap[0:CH, 0:64], s, i32sb[0:64, 0:64]), TE)
                    bk.wrote_group(tv)
                    bk.read("vector")
                    ev = run("vector", lambda e, c=c, d=dst, b=bk: e.tensor_copy(
                        d[:, c * 64:(c + 1) * 64], b.ap[0:CH, 0:64]), VE)
                    bk.consumed(VE, ev)
                run("vector", lambda e, c=c: e.tensor_sub(
                    biascol[:, c * 64:(c + 1) * 64], lndtT[:, c * 64:(c + 1) * 64],
                    laT[:, c * 64:(c + 1) * 64]), VE)
                bv = VE.n
                wait("scalar", VE, bv)
                run("scalar", lambda e, c=c: e.activation(
                    elatok[:, c * 64:(c + 1) * 64], laT[:, c * 64:(c + 1) * 64], AF.Exp), AC)
            wait("vector", VE, VE.n)
            st["dt_done_a"] = AC.n
            st["dt_done_v"] = VE.n
            st["dt_free"] = AC.n
            bk = banks[7]
            for c in range(NCH):
                wait("vector", TE, st.get("lcb_cons", 0))
                lv2 = run("vector", lambda e, c=c: e.tensor_scalar(
                    out=lcbsb[:, :], in0=onessb[0:64, 0:128],
                    scalar1=lat[:, c * CH + 127: c * CH + 128],
                    scalar2=None, op0=ALU.mult), VE)
                wait("tensor", VE, lv2)
                bk.begin_write()
                tv = run("tensor", lambda e, b=bk: e.matmul(
                    b.ap[:, 384:448], lcbsb[:, :], i32sb[0:64, 0:64],
                    start=True, stop=True), TE)
                st["lcb_cons"] = tv
                bk.wrote_group(tv)
                bk.read("scalar")
                av = run("scalar", lambda e, c=c, b=bk: e.activation(
                    elac[:, c * 64:(c + 1) * 64], b.ap[0:64, 384:448], AF.Exp), AC)
                bk.consumed(AC, av)
                if with_wcols:
                    bk.read("vector")
                    wait("vector", AC, av)
                    ev = run("vector", lambda e, c=c, b=bk: e.tensor_add(
                        wcols[:, c * 64:(c + 1) * 64], biascol[:, c * 64:(c + 1) * 64],
                        b.ap[:, 384:448]), VE)
                    bk.consumed(VE, ev)
                    wait("scalar", VE, ev)
                    run("scalar", lambda e, c=c: e.activation(
                        wcols[:, c * 64:(c + 1) * 64], wcols[:, c * 64:(c + 1) * 64],
                        AF.Exp), AC)
            st["cols_done"] = AC.n

        def emit_bct(blk, c, with_c0):
            bcbase = 32 * STW + 3 + c * CH
            bk = banks[7]
            wait("tensor", AC, st["convt_32"])
            wait("vector", TE, st.get("scan_te", 0))
            bk.begin_write()
            tv = run("tensor", lambda e, b=bk, o=bcbase: e.transpose(
                b.ap[:, 320:448], ar_stage[:, o:o + 128], i32sb[:, :]), TE)
            bk.wrote_group(tv)
            bk.read("vector")
            ev = run("vector", lambda e, b=bk: e.tensor_copy(bct[:, :], b.ap[:, 320:448]), VE)
            bk.consumed(VE, ev)
            st["bct_v"] = ev
            if with_c0:
                wait("tensor", VE, ev)
                bk.begin_write()
                tv = run("tensor", lambda e, b=bk: e.transpose(
                    b.ap[0:64, 320:448], bct[:, 64:128], i32sb[:, :]), TE)
                bk.wrote_group(tv)
                bk.read("vector")
                ev = run("vector", lambda e, b=bk: e.tensor_copy(
                    c0sb[:, :], b.ap[0:64, 320:448]), VE)
                bk.consumed(VE, ev)
                st["c0_v"] = ev

        def emit_xt_group(blk, c, g):
            xt = xt0 if g % 2 == 0 else xt1
            wait("scalar", TE, st.get(f"xtcons_{g % 2}", 0))
            wait("tensor", AC, max(st[f"convt_{g * 2}"], st[f"convt_{g * 2 + 1}"]))
            bk = banks[3]
            for q in range(2):
                ct = g * 2 + q
                base = ct * STW + 3 + c * CH
                bk.begin_write()
                tv = run("tensor", lambda e, b=bk, o=base: e.transpose(
                    b.ap[:, 0:128], ar_stage[:, o:o + 128], i32sb[:, :]), TE)
                bk.wrote_group(tv)
                bk.read("scalar")
                ev = run("scalar", lambda e, b=bk, q=q, xt=xt: e.copy(
                    xt[:, q * 128:(q + 1) * 128], b.ap[:, 0:128]), AC)
                bk.consumed(AC, ev)
            return xt, AC.n

        def xslice(xt, hh):
            o = (hh // 2) * 128 + (hh % 2) * 64
            return xt[:, o:o + 64]

        def emit_scan_chunk(blk, c, phase):
            main = phase == "main"
            emit_bct(blk, c, with_c0=main)
            sprev = s_a if st["sparity"] % 2 == 0 else s_b
            snext = s_b if st["sparity"] % 2 == 0 else s_a
            st["sparity"] += 1
            if main:
                gbk = banks[3]
                bcbase = 32 * STW + 3 + c * CH
                wait("tensor", VE, st["c0_v"])
                gbk.begin_write()
                gv = run("tensor", lambda e, b=gbk, o=bcbase: e.matmul(
                    b.ap[:, 0:128], ar_stage[0:64, o:o + 128], c0sb[:, :],
                    start=True, stop=True), TE)
                gbk.wrote_group(gv)
                gbk.read("vector")
                gev = run("vector", lambda e, b=gbk: e.tensor_copy(g_sb[:, :], b.ap[:, 0:128]), VE)
                gbk.consumed(VE, gev)
                st["g_sb_v"] = gev
            def emit_p4_mask(g, c=c):
                if g % 4 == 0:
                    q = g // 4
                    p4buf = p4A if q % 2 == 0 else p4B
                    wait("sync", sLA, st["sla_last"])
                    wait("sync", GP, st.get(f"p4c_{q % 2}", 0))
                    st["sla_last"] = dma(
                        "sync", p4buf[:, :].rearrange("p (h t) -> p h t", h=16),
                        dlas[q * 16:(q + 1) * 16, c * CH:(c + 1) * CH]
                        .rearrange("(o h) t -> o h t", o=1)
                        .broadcast_to([128, 16, 128]), sLA)
                    st[f"p4v_{q % 2}"] = st["sla_last"]
                p4buf = p4A if (g // 4) % 2 == 0 else p4B
                pm = p4m_a if g % 2 == 0 else p4m_b
                wait("gpsimd", sLA, st[f"p4v_{(g // 4) % 2}"])
                wait("gpsimd", AC, st.get(f"pmc_{g % 2}", 0))
                gv = run("gpsimd", lambda e, g=g, pm=pm, pb=p4buf: e.tensor_tensor(
                    pm[:, :], pb[:, (g % 4) * 512:(g % 4 + 1) * 512],
                    sunegsb[:, :], ALU.add), GP)
                st[f"pm_{g % 2}"] = gv
                if g % 4 == 3:
                    st[f"p4c_{(g // 4) % 2}"] = gv

            for g in range(16):
                xt, xtv = emit_xt_group(blk, c, g)
                dsb = banks[4 + (g % 2)]
                wait("tensor", AC, st["cols_done"])
                wait("tensor", VE, st["bct_v"])
                if main:
                    if g == 0:
                        emit_p4_mask(0)
                    ybk = banks[0 + (g % 2)]
                    cbk = banks[2 if g % 2 == 0 else 6]
                    cbk.begin_write()
                    for hh in range(4):
                        h = g * 4 + hh
                        run("tensor", lambda e, hh=hh, h=h, b=cbk, sp=sprev: e.matmul(
                            b.ap[:, hh * 64:(hh + 1) * 64], c0sb[:, :],
                            sp[:, h * 64:(h + 1) * 64],
                            start=(hh == 0), stop=(hh == 3)), TE if hh == 3 else None)
                    cbk.wrote_group(TE.n)
                    ybk.begin_write()
                    pm = p4m_a if g % 2 == 0 else p4m_b
                dsb.begin_write()
                for hh in range(4):
                    h = g * 4 + hh
                    par = hh % 2
                    ebf = (e_a, e_b)[par]
                    mbf = (m_a, m_b)[par]
                    bwb = bwsb if par == 0 else bwsb2
                    if main:
                        wait("scalar", GP, st[f"pm_{g % 2}"])
                        wait("scalar", GP, st.get(f"econs_g_{par}", 0))
                        wait("scalar", VE, st.get(f"econs_v_{par}", 0))
                        av = run("scalar", lambda e, h=h, c=c, hh=hh, ef=ebf, pm=pm: e.activation(
                            ef[:, :], pm[:, hh * 128:(hh + 1) * 128], AF.Exp,
                            bias=biascol[:, c * 64 + h: c * 64 + h + 1]), AC)
                        wait("gpsimd", AC, av)
                        wait("gpsimd", TE, st.get(f"mcons_{par}", 0))
                        wait("gpsimd", VE, st["g_sb_v"])
                        mv = run("gpsimd", lambda e, ef=ebf, mf=mbf: e.tensor_tensor(
                            mf[:, :], ef[:, :], g_sb[:, :], ALU.mult), GP)
                        st[f"econs_g_{par}"] = mv
                        st[f"pmc_{g % 2}"] = av
                        wait("vector", AC, av)
                        wait("vector", TE, st.get(f"bwc_{par}", 0))
                        bv = run("vector", lambda e, ef=ebf, bw=bwb: e.tensor_scalar(
                            out=bw[:, :], in0=bct[:, 0:64],
                            scalar1=ef[:, 127:128],
                            scalar2=None, op0=ALU.mult), VE)
                        st[f"econs_v_{par}"] = bv
                    else:
                        wait("vector", AC, st["cols_done"])
                        wait("vector", TE, st.get(f"bwc_{par}", 0))
                        bv = run("vector", lambda e, h=h, c=c, bw=bwb: e.tensor_scalar(
                            out=bw[:, :], in0=bct[:, 0:64],
                            scalar1=wcols[:, c * 64 + h: c * 64 + h + 1],
                            scalar2=None, op0=ALU.mult), VE)
                    if main:
                        wait("tensor", GP, mv)
                        wait("tensor", AC, xtv)
                        iv = run("tensor", lambda e, hh=hh, mf=mbf, b=ybk, xt=xt: e.matmul(
                            b.ap[:, hh * 64:(hh + 1) * 64], mf[:, :], xslice(xt, hh),
                            start=(hh == 0), stop=(hh == 3)), TE)
                        st[f"mcons_{par}"] = iv
                        ybk.wrote_group(iv)
                    wait("tensor", VE, bv)
                    wait("tensor", AC, xtv)
                    dv = run("tensor", lambda e, hh=hh, b=dsb, xt=xt, bw=bwb: e.matmul(
                        b.ap[0:64, hh * 64:(hh + 1) * 64], bw[:, :], xslice(xt, hh),
                        start=True, stop=True), TE)
                    dsb.wrote_group(dv)
                    st[f"bwc_{par}"] = dv
                if main:
                    if g < 15:
                        emit_p4_mask(g + 1)
                    ybk.read("vector")
                    cbk.read("scalar")
                    wait("vector", AC, st["z_done"])
                    wait("gpsimd", AC, st["z_done"])
                    for hh in range(4):
                        h = g * 4 + hh
                        par = hh % 2
                        ysl = ar_y[:, c * DI + h * 64: c * DI + (h + 1) * 64]
                        ysc = yscr0 if par == 0 else yscr1
                        ycb = ycr0 if par == 0 else ycr1
                        wait("vector", GP, st.get(f"yscons_{par}", 0))
                        run("vector", lambda e, hh=hh, h=h, xt=xt, ysc=ysc, b=ybk: e.scalar_tensor_tensor(
                            out=ysc[:, :], in0=xslice(xt, hh), scalar=drepsb[:, h:h + 1],
                            in1=b.ap[:, hh * 64:(hh + 1) * 64],
                            op0=ALU.mult, op1=ALU.add), VE)
                        wait("scalar", VE, st.get(f"ycbc_{par}", 0))
                        av = run("scalar", lambda e, hh=hh, h=h, c=c, b=cbk, ycb=ycb: e.activation(
                            ycb[:, :], b.ap[:, hh * 64:(hh + 1) * 64],
                            AF.Copy, bias=0.0,
                            scale=elatok[:, c * 64 + h: c * 64 + h + 1]), AC)
                        wait("vector", AC, av)
                        vv = run("vector", lambda e, ysc=ysc, ycb=ycb: e.tensor_tensor(
                            ysc[:, :], ysc[:, :], ycb[:, :], ALU.add), VE)
                        st[f"ycbc_{par}"] = vv
                        wait("gpsimd", VE, vv)
                        run("gpsimd", lambda e, ysl=ysl, ysc=ysc: e.tensor_tensor(
                            ysl, ysc[:, :], ysl, ALU.mult), GP)
                        st[f"yscons_{par}"] = GP.n
                    ybk.consumed(VE, VE.n)
                    cbk.consumed(AC, AC.n)
                dsb.read("vector")
                wait("vector", AC, st["cols_done"])
                for hh in range(4):
                    h = g * 4 + hh
                    run("vector", lambda e, hh=hh, h=h, c=c, b=dsb, sp=sprev, sn=snext: e.scalar_tensor_tensor(
                        out=sn[:, h * 64:(h + 1) * 64], in0=sp[:, h * 64:(h + 1) * 64],
                        scalar=elac[:, c * 64 + h: c * 64 + h + 1],
                        in1=b.ap[0:64, hh * 64:(hh + 1) * 64],
                        op0=ALU.mult, op1=ALU.add), VE)
                dsb.consumed(VE, VE.n)
                st[f"xtcons_{g % 2}"] = TE.n
            if main:
                st["y_done"] = GP.n
            st["scan_te"] = TE.n
            st["scan_ve"] = VE.n
            st["scan_ac"] = AC.n

        def emit_z_early(blk):
            wait("scalar", VE, st["uT_ready"])
            wait("scalar", AC, st.get("uT_ready_a", 0)) if False else None
            wait("scalar", TE, st.get("ynt_te", 0))
            wait("tensor", AC, st.get("uT_ready_a", 0))
            for zt in range(NXT):
                w = wA if zt % 2 == 0 else wB
                qe = "sync" if zt % 2 == 0 else "scalar"
                wait(qe, TE, st.get(f"wc{zt % 2}", 0))
                sw = sWA if zt % 2 == 0 else sWB
                lv = dma(qe, w[:, 0:NKT * 128],
                         dWT[:, zt * 2048:(zt + 1) * 2048], sw)
                bk = banks[4 + (zt % 2)]
                wait("tensor", sw, lv)
                bk.begin_write()
                for tt in range(2):
                    for k in range(NKT):
                        is_last = (tt == 1 and k == NKT - 1)
                        run("tensor", lambda e, k=k, tt=tt, w=w, b=bk: e.matmul(
                            b.ap[:, tt * 128:(tt + 1) * 128],
                            ar_ut[:, k * TB + tt * 128: k * TB + tt * 128 + 128],
                            w[:, k * 128:(k + 1) * 128],
                            start=(k == 0), stop=(k == NKT - 1)), TE if is_last else None)
                tv = TE.n
                st[f"wc{zt % 2}"] = tv
                bk.wrote_group(tv)
                bk.read("scalar")
                av = None
                for c in range(NCH):
                    av = run("scalar", lambda e, b=bk, c=c, zt=zt: e.activation(
                        ar_y[:, c * DI + zt * 128: c * DI + zt * 128 + 128],
                        b.ap[:, c * 128:(c + 1) * 128], AF.Silu), AC)
                bk.consumed(AC, av)
            st["z_done"] = AC.n

        def emit_gated_ynt(blk):
            wait("vector", GP, st["y_done"])
            wait("vector", TE, st["scan_te"])
            wait("vector", GP, st["halo_w_g"])
            wait("vector", AC, st["conv_done"])
            for c in range(NCH):
                for half in range(2):
                    run("vector", lambda e, c=c, half=half: e.scalar_tensor_tensor(
                        out=lnscr, in0=ar_y[:, c * DI + half * D: c * DI + (half + 1) * D],
                        scalar=1.0,
                        in1=ar_y[:, c * DI + half * D: c * DI + (half + 1) * D],
                        op0=ALU.mult, op1=ALU.mult,
                        accum_out=ssqcol[:, 2 * c + half: 2 * c + half + 1]), VE)
                wait("vector", VE, VE.n)
                vv = run("vector", lambda e, c=c: e.tensor_add(
                    ssqcol[:, 2 * c: 2 * c + 1], ssqcol[:, 2 * c: 2 * c + 1],
                    ssqcol[:, 2 * c + 1: 2 * c + 2]), VE)
                wait("scalar", VE, vv)
                run("scalar", lambda e, c=c: e.activation(
                    gcol[:, c:c + 1], ssqcol[:, 2 * c: 2 * c + 1], AF.Ln,
                    bias=cEPS, scale=1.0 / DI), AC)
                av = run("scalar", lambda e, c=c: e.activation(
                    gcol[:, c:c + 1], gcol[:, c:c + 1], AF.Exp, scale=-0.5), AC)
                wait("vector", AC, av)
                ynv = run("vector", lambda e, c=c: e.tensor_scalar(
                    out=ar_y[:, c * DI:(c + 1) * DI], in0=ar_y[:, c * DI:(c + 1) * DI],
                    scalar1=gcol[:, c:c + 1], scalar2=None, op0=ALU.mult), VE)
                wait("tensor", VE, ynv)
                for et in range(NXT):
                    bk = banks[4 + (et % 2)]
                    bk.begin_write()
                    tv = run("tensor", lambda e, c=c, et=et, b=bk: e.transpose(
                        b.ap[:, 0:128], ar_y[:, c * DI + et * 128: c * DI + (et + 1) * 128],
                        i32sb[:, :]), TE)
                    bk.wrote_group(tv)
                    bk.read("scalar")
                    ev = run("scalar", lambda e, c=c, et=et, b=bk: e.activation(
                        ynt_ap[:, et * TB + c * 128: et * TB + c * 128 + 128],
                        b.ap[:, 0:128], AF.Copy, bias=0.0,
                        scale=normsb[:, et:et + 1]), AC)
                    bk.consumed(AC, ev)
            st["ynt_done_a"] = AC.n
            st["ynt_te"] = TE.n
            st["ynt_done"] = VE.n

        def emit_outproj_x1(blk):
            r0 = ar_y.ap()[:, 0:2 * D]
            xin2 = ar_y.ap()[:, 2 * D:4 * D]
            wait("sync", VE, st["ynt_done"])
            wait("sync", TE, st["ynt_te"])
            for tt in range(2):
                sx2 = sX2A if tt == 0 else sX2B
                xv = dma("sync", xin2[:, tt * D:(tt + 1) * D],
                         dx[blk * TB + tt * 128: blk * TB + (tt + 1) * 128, :], sx2)
                mcol = colsb[:, 8 * tt + 0: 8 * tt + 1]
                rstd = colsb[:, 8 * tt + 2: 8 * tt + 3]
                wait("vector", sx2, xv)
                wait("vector", TE, st["ynt_te"])
                wait("vector", TE, st["uT_cons"])
                run("vector", lambda e, tt=tt, m=mcol, r=rstd, x2=xin2: e.tensor_scalar(
                    out=ar_u[:, :], in0=x2[:, tt * D:(tt + 1) * D],
                    scalar1=m, scalar2=r, op0=ALU.subtract, op1=ALU.mult), VE)
                run("vector", lambda e, tt=tt, x2=xin2, r0=r0: e.tensor_add(
                    r0[:, tt * D:(tt + 1) * D], x2[:, tt * D:(tt + 1) * D], ar_u[:, :]), VE)
            r0v = VE.n
            st["xin_free_v"] = r0v
            wait("tensor", AC, st["ynt_done_a"])
            for dmb in range(4):
                bk0 = banks[0]
                bk1 = banks[1]
                bk0.begin_write()
                bk1.begin_write()
                for kg in range(8):
                    w = wA if kg % 2 == 0 else wB
                    sw = sWA if kg % 2 == 0 else sWB
                    qe = "sync" if kg % 2 == 0 else "scalar"
                    wait(qe, TE, st.get(f"wc{kg % 2}", 0))
                    lv = dma(qe, w[:, 0:2048],
                             dWoT[:, (dmb * 8 + kg) * 2048:(dmb * 8 + kg + 1) * 2048], sw)
                    wait("tensor", sw, lv)
                    for kk in range(4):
                        k = kg * 4 + kk
                        for tt, bk in ((0, bk0), (1, bk1)):
                            last_of_group = (kk == 3 and tt == 1)
                            run("tensor", lambda e, k=k, kk=kk, tt=tt, w=w, b=bk: e.matmul(
                                b.ap[:, 0:512],
                                ynt_ap[:, k * TB + tt * 128: k * TB + tt * 128 + 128],
                                w[:, kk * 512:(kk + 1) * 512],
                                start=(k == 0), stop=(k == 31)),
                                TE if last_of_group else None)
                    st[f"wc{kg % 2}"] = TE.n
                tv = TE.n
                bk0.wrote_group(tv)
                bk1.wrote_group(tv)
                for tt, bk in ((0, bk0), (1, bk1)):
                    bk.read("vector")
                    wait("vector", VE, r0v)
                    wait("vector", sOUT, st.get("x1_free", 0))
                    ev = run("vector", lambda e, tt=tt, dmb=dmb, b=bk, r0=r0: e.tensor_add(
                        ar_x1[:, tt * D + dmb * 512: tt * D + (dmb + 1) * 512],
                        r0[:, tt * D + dmb * 512: tt * D + (dmb + 1) * 512],
                        b.ap[:, 0:512]), VE)
                    bk.consumed(VE, ev)
            st["x1_done"] = VE.n
            st["opj_te"] = TE.n

        def emit_ffn_out(blk):
            qd = q_done_val()
            if qd:
                wait("sync", sPO0, qd[0])
                wait("sync", sPO1, qd[1])
            aq = aqT()
            xq = xqT()
            wait("vector", TE, st["opj_te"])
            for tt in range(2):
                x1t = ar_x1[:, tt * D:(tt + 1) * D]
                mcol = colsb[:, 8 * tt + 0: 8 * tt + 1]
                vcol = colsb[:, 8 * tt + 1: 8 * tt + 2]
                rstd = colsb[:, 8 * tt + 2: 8 * tt + 3]
                wait("vector", VE, st["x1_done"])
                rv = run("vector", lambda e, x=x1t, o=mcol: e.reduce_sum(o, x, axis=AXX), VE)
                wait("vector", VE, rv)
                mv0 = run("vector", lambda e, o=mcol: e.tensor_scalar_mul(o, o, 1.0 / D), VE)
                wait("vector", VE, mv0)
                wait("vector", TE, st.get("lnu_cons", 0))
                wait("vector", AC, st.get("lnu_cons_a", 0))
                run("vector", lambda e, x=x1t, m=mcol: e.tensor_scalar(
                    out=ar_u[:, :], in0=x, scalar1=m, scalar2=None, op0=ALU.subtract), VE)
                vv = run("vector", lambda e, v=vcol: e.scalar_tensor_tensor(
                    out=ln2scr, in0=ar_u[:, :], scalar=1.0, in1=ar_u[:, :],
                    op0=ALU.mult, op1=ALU.mult, accum_out=v), VE)
                wait("scalar", VE, vv)
                run("scalar", lambda e, v=vcol, r=rstd: e.activation(
                    r, v, AF.Ln, bias=cEPS, scale=1.0 / D), AC)
                av = run("scalar", lambda e, r=rstd: e.activation(r, r, AF.Exp, scale=-0.5), AC)
                st["colsb_ac"] = av
                wait("vector", AC, av)
                lnv = run("vector", lambda e, r=rstd: e.tensor_scalar(
                    out=ar_u[:, :], in0=ar_u[:, :], scalar1=r, scalar2=None,
                    op0=ALU.mult), VE)
                wait("tensor", VE, lnv)
                for k in range(NKT):
                    bk = banks[6 + (k % 2)]
                    bk.begin_write()
                    tv = run("tensor", lambda e, k=k, b=bk: e.transpose(
                        b.ap[:, 0:128], ar_u[:, k * 128:(k + 1) * 128], i32sb[:, :]), TE)
                    bk.wrote_group(tv)
                    bk.read("vector")
                    ev = run("vector", lambda e, k=k, tt=tt, b=bk, xq=xq: e.tensor_copy(
                        xq[:, k * TB + tt * 128: k * TB + tt * 128 + 128], b.ap[:, 0:128]), VE)
                    bk.consumed(VE, ev)
                st["lnu_cons"] = TE.n
            xqv = VE.n
            wait("tensor", VE, xqv)
            w8A = wA.ap().bitcast(FP8)
            w8B = wB.ap().bitcast(FP8)

            def emit_aq_transposes(pend):
                pv, hb, tt = pend
                wait("tensor", VE, pv)
                for q in range(4):
                    k2 = hb * 4 + q
                    bk = banks[4 + (q % 2)]
                    bk.begin_write()
                    pb = bk.ap.bitcast(BF16)
                    tv = run("tensor", lambda e, q=q, pb=pb: e.transpose(
                        pb[:, 0:128], prodsb[:, q * 128:(q + 1) * 128], ibfsb[:, :]), TE)
                    bk.wrote_group(tv)
                    bk.read("vector")
                    ev = run("vector", lambda e, k2=k2, tt=tt, pb=pb, aq=aq: e.tensor_copy(
                        aq[:, k2 * TB + tt * 128: k2 * TB + tt * 128 + 128],
                        pb[:, 0:128]), VE)
                    bk.consumed(VE, ev)
                st["prod_cons"] = TE.n

            w8C = ar_ut.ap().bitcast(FP8)[:, 0:8192]
            w8D = ar_ut.ap().bitcast(FP8)[:, 8192:16384]
            pending = None
            for hb in range(16):
                if hb % 2 == 0:
                    wg8, wu8, sg, su = w8A, w8B, sWA, sWB
                    kgk, kuk = "wc0", "wc1"
                else:
                    wg8, wu8, sg, su = w8C, w8D, sWC, sWD
                    kgk, kuk = "wcC", "wcD"
                if hb == 1:
                    wait("sync", TE, st["ut_free"])
                    wait("scalar", TE, st["ut_free"])
                wait("sync", TE, st.get(kgk, 0))
                lv1 = dma("sync", wg8[:, 0:NKT * 512],
                          dg0q[:, hb * 8192:(hb + 1) * 8192], sg)
                wait("scalar", TE, st.get(kuk, 0))
                lv2 = dma("scalar", wu8[:, 0:NKT * 512],
                          dg1q[:, hb * 8192:(hb + 1) * 8192], su)
                wait("tensor", sg, lv1)
                wait("tensor", su, lv2)
                for tt in range(2):
                    p2 = (hb * 2 + tt) % 2
                    gbk = banks[0 + p2]
                    ubk = banks[2 + p2]
                    gbk.begin_write()
                    ubk.begin_write()
                    for k in range(NKT):
                        run("tensor", lambda e, k=k, tt=tt, b=gbk, w=wg8, xq=xq: e.matmul(
                            b.ap[:, 0:512],
                            xq[:, k * TB + tt * 128: k * TB + tt * 128 + 128],
                            w[:, k * 512:(k + 1) * 512],
                            start=(k == 0), stop=(k == NKT - 1)), None)
                    for k in range(NKT):
                        is_last = k == NKT - 1
                        run("tensor", lambda e, k=k, tt=tt, b=ubk, w=wu8, xq=xq: e.matmul(
                            b.ap[:, 0:512],
                            xq[:, k * TB + tt * 128: k * TB + tt * 128 + 128],
                            w[:, k * 512:(k + 1) * 512],
                            start=(k == 0), stop=(k == NKT - 1)), TE if is_last else None)
                    tv = TE.n
                    if tt == 1:
                        st[kgk] = tv
                        st[kuk] = tv
                    gbk.wrote_group(tv)
                    ubk.wrote_group(tv)
                    if pending is not None:
                        emit_aq_transposes(pending)
                    gbk.read("vector")
                    ubk.read("vector")
                    run("vector", lambda e, b=gbk: e.tensor_copy(gatesb[:, :], b.ap[:, 0:512]), VE)
                    vv = run("vector", lambda e, b=ubk: e.tensor_copy(upsb[:, :], b.ap[:, 0:512]), VE)
                    gbk.consumed(VE, vv)
                    ubk.consumed(VE, vv)
                    wait("scalar", VE, vv)
                    av = run("scalar", lambda e: e.activation(
                        gatesb[:, :], gatesb[:, :], AF.Gelu_apprx_tanh), AC)
                    wait("vector", AC, av)
                    wait("vector", TE, st.get("prod_cons", 0))
                    pv = run("vector", lambda e: e.tensor_tensor(
                        prodsb[:, :], gatesb[:, :], upsb[:, :], ALU.mult), VE)
                    pending = (pv, hb, tt)
            emit_aq_transposes(pending)
            st["ut_free"] = TE.n
            aqv = VE.n
            wait("tensor", VE, aqv)
            for tt in range(2):
                for dmb in range(4):
                    banks[tt * 4 + dmb].begin_write()
            for ag in range(16):
                w8 = w8A if ag % 2 == 0 else w8B
                sw = sWA if ag % 2 == 0 else sWB
                qe = "sync" if ag % 2 == 0 else "scalar"
                wait(qe, TE, st.get(f"wc{ag % 2}", 0))
                lv = dma(qe, w8[:, 0:8192],
                         dwlq[:, ag * 8192:(ag + 1) * 8192], sw)
                wait("tensor", sw, lv)
                for a in range(4):
                    k2 = ag * 4 + a
                    for tt in range(2):
                        for dmb in range(4):
                            bk = banks[tt * 4 + dmb]
                            is_last = (a == 3 and tt == 1 and dmb == 3)
                            run("tensor", lambda e, k2=k2, a=a, tt=tt, dmb=dmb, w8=w8, b=bk, aq=aq: e.matmul(
                                b.ap[:, 0:512],
                                aq[:, k2 * TB + tt * 128: k2 * TB + tt * 128 + 128],
                                w8[:, a * 2048 + dmb * 512: a * 2048 + (dmb + 1) * 512],
                                start=(k2 == 0), stop=(k2 == 63)),
                                TE if is_last else None)
                st[f"wc{ag % 2}"] = TE.n
            tvf = TE.n
            for tt in range(2):
                for dmb in range(4):
                    bk = banks[tt * 4 + dmb]
                    bk.wrote_group(tvf)
                    bk.read("vector")
                    run("vector", lambda e, b=bk: e.tensor_copy(prodsb[:, :], b.ap[:, 0:512]), VE)
                    ev = run("vector", lambda e, tt=tt, dmb=dmb: e.tensor_tensor(
                        ar_x1[:, tt * D + dmb * 512: tt * D + (dmb + 1) * 512],
                        ar_x1[:, tt * D + dmb * 512: tt * D + (dmb + 1) * 512],
                        prodsb[:, :], ALU.add), VE)
                    bk.consumed(VE, ev)
            x2v = VE.n
            wait("scalar", VE, x2v)
            sv = dma("scalar",
                     dout[blk * TB:(blk + 1) * TB, :].rearrange("(a p) d -> p a d", p=128),
                     ar_x1[:, :].rearrange("p (a d) -> p a d", a=2), sOUT)
            st["x1_free"] = sv

        import os as _os
        _skip_q = bool(int(_os.environ.get("KBISECT_SKIP_Q", "0")))
        if _skip_q:
            qstate["i"] = len(q_jobs)
        _rec_stop = _os.environ.get("KBISECT_REC_STOP", "")
        if not skip_recompute:
            for blk in range(nb):
                emit_q_jobs(12)
                emit_ln_uT(dxp, blk)
                if _rec_stop == "ln":
                    continue
                emit_inproj_xbcdt(blk, "head")
                emit_dt_head(blk)
                emit_inproj_xbcdt(blk, "rest")
                if _rec_stop == "inproj":
                    continue
                emit_dt_tail(blk, with_wcols=True)
                emit_conv(blk)
                if _rec_stop == "conv":
                    continue
                if _rec_stop == "dt":
                    continue
                for c in range(NCH):
                    emit_scan_chunk(blk, c, "recompute")
                st["stagefree_v"] = st["scan_ve"]
                st["stagefree_a"] = st["conv_done"]
                st["stagefree_g"] = st["halo_w_g"]
                st["stagefree_t"] = st["scan_te"]
                st["ut_free"] = TE.n
        else:
            run("vector", lambda e: e.memset(s_b[:, :], 0.0), VE)
            st["sparity"] = 0
        if not _skip_q:
            emit_q_jobs(len(q_jobs))
        sfin = s_a if st["sparity"] % 2 == 0 else s_b
        sini = s_b if st["sparity"] % 2 == 0 else s_a
        wait("vector", TE, TE.n)
        run("vector", lambda e, sf=sfin, si=sini: e.tensor_scalar(
            out=si[:, :], in0=sf[:, :], scalar1=flagsb[:, :], scalar2=None,
            op0=ALU.mult), VE)
        st["sparity"] = 0 if sini is s_a else 1
        tap("s_init", sini[:, :])

        wait("vector", GP, GP.n)
        wait("vector", sPO0, sPO0.n)
        wait("vector", sPO1, sPO1.n)
        wait("scalar", GP, GP.n)
        wait("scalar", sPO0, sPO0.n)
        wait("scalar", sPO1, sPO1.n)

        if True:
            tap("uT", ar_ut[:, :])
            tap("stage", ar_stage[:, 0:NET * STW])
            tap("y", ar_y.ap()[:, 0:2 * DI])
            tap("ynt", ynt_ap[:, 0:NXT * TB])
            tap("x1", ar_x1[:, :])
            tap("dt", dtval[:, :])
            tap("lat", lat[:, :])
            tap("laT", laT[:, :])
            tap("elac", elac[:, :])
            tap("dtraw", dtraw[:, :])
        for blk in range(nb):
            emit_ln_uT(dx, blk)
            if dev_stop == "ln":
                continue
            emit_inproj_xbcdt(blk, "head")
            emit_dt_head(blk)
            emit_inproj_xbcdt(blk, "rest")
            if dev_stop == "inproj":
                continue
            emit_dt_tail(blk, with_wcols=False)
            emit_conv(blk)
            if dev_stop == "conv":
                continue
            emit_z_early(blk)
            st["ut_free"] = TE.n
            if dev_stop != "dt":
                for c in range(NCH):
                    emit_scan_chunk(blk, c, "main")
                if dev_stop != "scan":
                    emit_gated_ynt(blk)
                    if dev_stop != "gated":
                        emit_outproj_x1(blk)
                        if dev_stop != "opj":
                            emit_ffn_out(blk)
            st["stagefree_v"] = VE.n
            st["stagefree_a"] = AC.n
            st["stagefree_g"] = GP.n
            st["stagefree_t"] = TE.n

        if debug_tap is not None:
            names = (debug_tap,) if isinstance(debug_tap, str) else tuple(debug_tap)
            wait("scalar", VE, VE.n)
            wait("scalar", TE, TE.n)
            wait("scalar", AC, AC.n)
            wait("scalar", GP, GP.n)
            off = 0
            dbg_layout = {}
            for nm in names:
                src = taps[nm]
                pw = src.shape[1] if len(src.shape) == 2 else int(np.prod(src.shape[1:]))
                prt = src.shape[0]
                dbg_layout[nm] = (off, prt, pw)
                if sDBG.n:
                    wait("scalar", sDBG, sDBG.n)
                dma("scalar", ddbg[0:prt, off:off + pw], src, sDBG)
                off += pw
            assert off <= dbg_cols, f"debug taps too wide: {off}"
            build_nc.dbg_layout = dbg_layout

        wait("sync", sOUT, sOUT.n)
        wait("sync", sPO0, sPO0.n)
        wait("sync", sPO1, sPO1.n)
        wait("sync", sDBG, sDBG.n)

        with nc.Block() as block:
            block.sync(lambda e: [f(e) for f in streams["sync"]])
            block.tensor(lambda e: [f(e) for f in streams["tensor"]])
            block.vector(lambda e: [f(e) for f in streams["vector"]])
            block.scalar(lambda e: [f(e) for f in streams["scalar"]])
            block.gpsimd(lambda e: [f(e) for f in streams["gpsimd"]])

    return nc


def _host_prep(inputs):
    f32 = np.float32
    x = np.asarray(inputs["L0_attn_hidden_states"], dtype=f32)
    WT = np.ascontiguousarray(np.asarray(inputs["L0_attn_in_proj_w"], f32).T)
    WoT = np.ascontiguousarray(np.asarray(inputs["L0_attn_out_proj_w"], f32).T)
    wg = np.asarray(inputs["L0_ffn_w_gating"], f32)
    wtb = []
    for et in range(66):
        ec = 128 if et < 65 else 64
        wtb.append(WT[:, et * 128: et * 128 + ec]
                   .reshape(16, 128, ec).transpose(1, 0, 2).reshape(128, 16 * ec))
    WTb = np.ascontiguousarray(np.concatenate(wtb, axis=1))
    wotb = []
    for dmb in range(4):
        for kg in range(8):
            wotb.append(WoT[kg * 512:(kg + 1) * 512, dmb * 512:(dmb + 1) * 512]
                        .reshape(4, 128, 512).transpose(1, 0, 2).reshape(128, 2048))
    WoTb = np.ascontiguousarray(np.concatenate(wotb, axis=1))
    shared = {
        "WTb": WTb, "WoTb": WoTb,
        "Wg0": np.ascontiguousarray(wg[0]), "Wg1": np.ascontiguousarray(wg[1]),
        "Wl": np.ascontiguousarray(np.asarray(inputs["L0_ffn_w_linear"], f32)),
        "convw": np.ascontiguousarray(
            np.asarray(inputs["L0_attn_conv_w"], f32).reshape(NCT, 128, 4).transpose(1, 0, 2)),
        "convb": np.ascontiguousarray(
            np.asarray(inputs["L0_attn_conv_b"], f32).reshape(NCT, 128).T),
        "rms": np.ascontiguousarray(
            np.asarray(inputs["L0_attn_rms_weight"], f32).reshape(NKT, 128).T),
        "normw": np.ascontiguousarray(
            np.asarray(inputs["L0_attn_norm_w"], f32).reshape(NXT, 128).T),
        "dtb": np.asarray(inputs["L0_attn_dt_bias"], f32).reshape(64, 1).copy(),
        "alog": np.asarray(inputs["L0_attn_A_log"], f32).reshape(64, 1).copy(),
        "Drep": np.ascontiguousarray(np.broadcast_to(
            np.asarray(inputs["L0_attn_D"], f32).reshape(1, 64), (128, 64))),
        "ident32": np.eye(128, dtype=f32),
        "identbf": np.eye(128, dtype=f32).astype(ml_dtypes.bfloat16),
        "onesc": np.ones((128, 128), dtype=f32),
        "sunegt4": np.ascontiguousarray(np.tile(
            np.where(np.arange(128)[None, :] < np.arange(128)[:, None],
                     f32(-30000.0), f32(0.0)).astype(f32), (1, 4))),
        "epsone": np.ascontiguousarray(
            np.broadcast_to(np.array([EPS, 1.0], dtype=f32), (128, 2))),
    }
    per_core = []
    zeros = np.zeros((2048, D), dtype=f32)
    for c in range(8):
        b, h = c // 2, c % 2
        m = dict(shared)
        m["x_own"] = np.ascontiguousarray(x[b, h * 2048:(h + 1) * 2048, :])
        m["x_prev"] = np.ascontiguousarray(x[b, 0:2048, :]) if h == 1 else zeros
        m["flag"] = np.full((64, 1), f32(h), dtype=f32)
        per_core.append(m)
    return per_core


def kernel(**inputs):
    per_core = _host_prep(inputs)
    key = (DEV_NB, DEV_SKIP_RECOMPUTE, DEBUG_TAP, DEV_STOP)
    if key not in _BUILD_CACHE:
        _BUILD_CACHE[key] = build_nc(DEV_NB, DEV_SKIP_RECOMPUTE, DEBUG_TAP, DEV_STOP)
    nc = _BUILD_CACHE[key]
    res = run_bass_kernel_spmd(nc, per_core, core_ids=list(range(8)))
    out = np.empty((4, 4096, D), dtype=np.float32)
    for c in range(8):
        b, h = c // 2, c % 2
        out[b, h * 2048:(h + 1) * 2048, :] = np.asarray(res.results[c]["out"])
    if DEBUG_TAP is not None:
        kernel.dbg = [np.asarray(r["dbg"]) for r in res.results]
    return out


# version 14
# speedup vs baseline: 1.3352x; 1.3352x over previous
import contextlib
import numpy as np
import ml_dtypes
import concourse.bass as bass
import concourse.mybir as mybir
from concourse.bass_utils import run_bass_kernel_spmd

F32 = mybir.dt.float32
BF16 = mybir.dt.bfloat16
FP8 = mybir.dt.float8e4
AF = mybir.ActivationFunctionType
ALU = mybir.AluOpType
AXX = mybir.AxisListType.X

D = 2048
E_ALL = 8384
DI = 4096
NH = 64
NKT = 16
TB = 256
NB = 8
CH = 128
NCH = TB // CH
NXT = 32
NCT = 33
NET = 34
STW = 259
EPS = 1e-5

DEV_NB = None
DEV_SKIP_RECOMPUTE = False
DEV_STOP = None
DEBUG_TAP = None

_BUILD_CACHE = {}


def build_nc(dev_nb=None, skip_recompute=False, debug_tap=None, dev_stop=None):
    nb = dev_nb or NB
    nc = bass.Bass("TRN2", target_bir_lowering=False)

    dx = nc.dram_tensor("x_own", [2048, D], F32, kind="ExternalInput").ap()
    dxp = nc.dram_tensor("x_prev", [2048, D], F32, kind="ExternalInput").ap()
    dflag = nc.dram_tensor("flag", [64, 1], F32, kind="ExternalInput").ap()
    dWT = nc.dram_tensor("WTb", [128, 66 * 2048 - 1024], F32, kind="ExternalInput").ap()
    dWoT = nc.dram_tensor("WoTb", [128, 32 * 2048], F32, kind="ExternalInput").ap()
    dWg0 = nc.dram_tensor("Wg0", [D, 8192], F32, kind="ExternalInput").ap()
    dWg1 = nc.dram_tensor("Wg1", [D, 8192], F32, kind="ExternalInput").ap()
    dWl = nc.dram_tensor("Wl", [8192, D], F32, kind="ExternalInput").ap()
    dcw = nc.dram_tensor("convw", [128, NCT, 4], F32, kind="ExternalInput").ap()
    dcb = nc.dram_tensor("convb", [128, NCT], F32, kind="ExternalInput").ap()
    drms = nc.dram_tensor("rms", [128, NKT], F32, kind="ExternalInput").ap()
    dnorm = nc.dram_tensor("normw", [128, NXT], F32, kind="ExternalInput").ap()
    ddtb = nc.dram_tensor("dtb", [64, 1], F32, kind="ExternalInput").ap()
    dalog = nc.dram_tensor("alog", [64, 1], F32, kind="ExternalInput").ap()
    dDrep = nc.dram_tensor("Drep", [128, 64], F32, kind="ExternalInput").ap()
    dI32 = nc.dram_tensor("ident32", [128, 128], F32, kind="ExternalInput").ap()
    dIbf = nc.dram_tensor("identbf", [128, 128], BF16, kind="ExternalInput").ap()
    dones = nc.dram_tensor("onesc", [128, 128], F32, kind="ExternalInput").ap()
    dsuneg = nc.dram_tensor("sunegt4", [128, 512], F32, kind="ExternalInput").ap()
    dlas = nc.dram_tensor("lascr", [64, 256], F32, kind="Internal").ap()
    depsone = nc.dram_tensor("epsone", [128, 2], F32, kind="ExternalInput").ap()
    dout = nc.dram_tensor("out", [2048, D], F32, kind="ExternalOutput").ap()
    dg0q = nc.dram_tensor("g0q", [128, 16 * 8192], FP8, kind="Internal").ap()
    dg1q = nc.dram_tensor("g1q", [128, 16 * 8192], FP8, kind="Internal").ap()
    dwlq = nc.dram_tensor("wlq", [128, 64 * 2048], FP8, kind="Internal").ap()
    ddbg = None
    dbg_cols = 40960
    if debug_tap is not None:
        ddbg = nc.dram_tensor("dbg", [128, dbg_cols], F32, kind="ExternalOutput").ap()

    streams = {k: [] for k in ("tensor", "vector", "scalar", "gpsimd", "sync")}

    class Sem:
        __slots__ = ("h", "n")

        def __init__(self, h):
            self.h = h
            self.n = 0

    def em(eng, fn):
        streams[eng].append(fn)

    def wait(eng, sem, val):
        if val and val > 0:
            em(eng, lambda e, s=sem.h, v=val: e.wait_ge(s, v))

    _SERIAL_ENGS = ("vector", "scalar", "gpsimd")

    def run(eng, builder, sem=None):
        if eng in _SERIAL_ENGS:
            own = ENG_SEM[eng]
            if sem is None:
                sem = own
            if own.n:
                em(eng, lambda e, s=own.h, v=own.n: e.wait_ge(s, v))
        if sem is None:
            em(eng, builder)
            return None
        sem.n += 1
        em(eng, lambda e, b=builder, s=sem.h: b(e).then_inc(s, 1))
        return sem.n

    def dma(qeng, out_ap, in_ap, sem):
        sem.n += 16
        em(qeng, lambda e, o=out_ap, i=in_ap, s=sem.h: e.dma_start(out=o, in_=i).then_inc(s, 16))
        return sem.n

    ENG_SEM = {}

    class Bank:
        __slots__ = ("ap", "cons", "wrote")

        def __init__(self, ap):
            self.ap = ap
            self.cons = []
            self.wrote = None

        def begin_write(self, eng="tensor"):
            for s, v in self.cons:
                wait(eng, s, v)
            self.cons = []

        def wrote_group(self, val):
            self.wrote = (ENG_SEM["tensor"], val)

        def read(self, eng):
            if self.wrote is not None:
                wait(eng, self.wrote[0], self.wrote[1])

        def consumed(self, sem, val):
            self.cons.append((sem, val))

    with contextlib.ExitStack() as ctx:
        def sb(name, shape, dt=F32):
            return ctx.enter_context(nc.sbuf_tensor(name, shape, dt))

        ar_stage = sb("stage", [128, NET * STW])
        ar_ut = sb("ut", [128, NKT * TB])
        ar_y = sb("ybuf", [128, 2 * DI])
        ar_x1 = sb("x1b", [128, 2 * D])
        ar_u = sb("ubuf", [128, D])
        wA = sb("wbufA", [128, 2048])
        wB = sb("wbufB", [128, 2048])
        s_a = sb("s_a", [64, DI])
        s_b = sb("s_b", [64, DI])
        xt0 = sb("xt0", [128, 512])
        xt1 = sb("xt1", [128, 512])
        acc0 = sb("acc0", [128, TB])
        acc1 = sb("acc1", [128, TB])
        accg = sb("accg", [128, TB])
        acc2 = sb("acc2", [128, TB])
        accg2 = sb("accg2", [128, TB])
        yscr0 = sb("yscr0", [128, 64])
        yscr1 = sb("yscr1", [128, 64])
        ycr0 = sb("ycr0", [128, 64])
        ycr1 = sb("ycr1", [128, 64])
        bct = sb("bct", [128, 128])
        g_sb = sb("g_sb", [128, 128])
        c0sb = sb("c0sb", [64, 128])
        e_a = sb("e_a", [128, 128])
        e_b = sb("e_b", [128, 128])
        p4A = sb("p4A", [128, 2048])
        p4B = sb("p4B", [128, 2048])
        p4m_a = sb("p4m_a", [128, 512])
        p4m_b = sb("p4m_b", [128, 512])
        m_a = sb("m_a", [128, 128])
        m_b = sb("m_b", [128, 128])
        bwsb = sb("bwsb", [128, 64])
        bwsb2 = sb("bwsb2", [128, 64])
        dtraw = sb("dtraw", [64, TB])
        dtval = sb("dtval", [64, TB])
        lndt = sb("lndt", [64, TB])
        dta = sb("dta", [64, TB])
        lat = sb("lat", [64, TB])
        laT = sb("laT", [128, NCH * 64])
        lndtT = sb("lndtT", [128, NCH * 64])
        biascol = sb("biascol", [128, NCH * 64])
        elatok = sb("elatok", [128, NCH * 64])
        elac = sb("elac", [64, NCH * 64])
        wcols = sb("wcols", [128, NCH * 64])
        lcbsb = sb("lcbsb", [64, 128])
        colsb = sb("colsb", [128, 16])
        gcol = sb("gcol", [128, 4])
        ssqcol = sb("ssqcol", [128, 2 * NXT + 2])
        halo3 = sb("halo3", [128, NCT * 3])
        gatesb = sb("gatesb", [128, 512], BF16)
        upsb = sb("upsb", [128, 512], BF16)
        prodsb = sb("prodsb", [128, 512], BF16)
        cwsb = sb("cwsb", [128, NCT * 4])
        cbsb = sb("cbsb", [128, NCT])
        rmssb = sb("rmssb", [128, NKT])
        normsb = sb("normsb", [128, NXT])
        dtbsb = sb("dtbsb", [64, 1])
        alogsb = sb("alogsb", [64, 1])
        acolsb = sb("acolsb", [64, 1])
        flagsb = sb("flagsb", [64, 1])
        drepsb = sb("drepsb", [128, 64])
        i32sb = sb("i32sb", [128, 128])
        ibfsb = sb("ibfsb", [128, 128], BF16)
        onessb = sb("onessb", [128, 128])
        sunegsb = sb("sunegsb", [128, 512])
        epsonesb = sb("epsonesb", [128, 2])

        st8 = ar_stage.ap().bitcast(FP8)
        aqT = lambda: st8[:, 0:64 * TB]
        xqT = lambda: st8[:, 64 * TB:64 * TB + NKT * TB]
        ln2scr = ar_stage.ap()[:, 5120:5120 + D]
        ynt_ap = ar_stage.ap()
        lnscr = ar_x1.ap()[:, 0:D]
        xin1 = ar_y.ap()[:, 3 * D:4 * D]
        yq = ar_y.ap()
        yq8 = ar_y.ap().bitcast(FP8)

        PS = [ctx.enter_context(nc.psum_tensor(f"ps{i}", [128, 512], F32)) for i in range(8)]
        banks = [Bank(PS[i].ap()) for i in range(8)]

        sem_names = ("LD", "SP", "TE", "VE", "AC", "GP",
                     "sX", "sX2A", "sX2B", "sWA", "sWB",
                     "sQ0", "sQ1", "sPO0", "sPO1", "sOUT", "sDBG", "sLA",
                     "sWC", "sWD")
        SEMH = {k: ctx.enter_context(nc.semaphore(k)) for k in sem_names}
        (LD, SP, TE, VE, AC, GP, sX, sX2A, sX2B, sWA, sWB,
         sQ0, sQ1, sPO0, sPO1, sOUT, sDBG, sLA,
         sWC, sWD) = (Sem(SEMH[k]) for k in sem_names)
        ENG_SEM.update(tensor=TE, vector=VE, scalar=AC, gpsimd=GP, sync=LD)

        taps = {}

        def tap(name, ap):
            taps[name] = ap

        def tpose(eng_unused, bank_slice, in_ap):
            p = in_ap.shape[0]
            return lambda e, b=bank_slice, i=in_ap, p=p: e.transpose(b, i, i32sb[0:p, 0:p])

        for dst, src in ((cwsb, dcw.rearrange("p c k -> p (c k)")), (cbsb, dcb[:, :]),
                         (rmssb, drms[:, :]), (normsb, dnorm[:, :]), (dtbsb, ddtb[:, :]),
                         (alogsb, dalog[:, :]), (flagsb, dflag[:, :]), (drepsb, dDrep[:, :]),
                         (i32sb, dI32[:, :]), (ibfsb, dIbf[:, :]), (onessb, dones[:, :]),
                         (sunegsb, dsuneg[:, :]), (epsonesb, depsone[:, :])):
            if LD.n:
                wait("sync", LD, LD.n)
            v0 = dma("sync", dst[:, :], src, LD)
        cEPS = epsonesb[:, 0:1]
        cONE = epsonesb[:, 1:2]
        wait("scalar", LD, v0)
        run("scalar", lambda e: e.activation(acolsb[:, :], alogsb[:, :], AF.Exp), AC)
        wait("vector", AC, AC.n)
        run("vector", lambda e: e.tensor_scalar_mul(acolsb[:, :], acolsb[:, :], -1.0), VE)
        run("vector", lambda e: e.memset(s_a[:, :], 0.0), VE)
        run("vector", lambda e: e.memset(halo3[:, :], 0.0), VE)
        run("vector", lambda e: e.memset(colsb[:, :], 0.0), VE)
        run("vector", lambda e: e.memset(ssqcol[:, :], 0.0), VE)
        run("vector", lambda e: e.memset(gcol[:, :], 0.0), VE)
        for t_ in (ar_stage, ar_ut, ar_y, ar_x1, ar_u, wA, wB, s_b):
            run("vector", lambda e, t_=t_: e.memset(t_[:, :], 0.0), VE)
        v_init = VE.n

        q_jobs = []
        dg0qv = dg0q.rearrange("p (hb w) -> p hb w", w=8192)
        dg1qv = dg1q.rearrange("p (hb w) -> p hb w", w=8192)
        for k in range(NKT):
            for wsrc, wdstv in ((dWg0, dg0qv), (dWg1, dg1qv)):
                q_jobs.append([(wsrc[k * 128:(k + 1) * 128, j * 2048:(j + 1) * 2048],
                                wdstv[:, 4 * j: 4 * j + 4, k * 512:(k + 1) * 512])
                               for j in range(4)])
        wlr = dWl.rearrange("(a p) d -> p a d", p=128)
        for k in range(16):
            q_jobs.append([(wlr[:, k * 4 + j, :],
                            dwlq[:, (k * 4 + j) * 2048:(k * 4 + j + 1) * 2048]
                            .rearrange("p (u h) -> p u h", u=4))
                           for j in range(4)])
        qstate = {"i": 0, "use": {0: (0, 0), 1: (0, 0)}}

        def emit_q_jobs(count):
            for _ in range(count):
                i = qstate["i"]
                if i >= len(q_jobs):
                    return
                qstate["i"] += 1
                if i == 0:
                    wait("sync", VE, v_init)
                for j, (src, dst) in enumerate(q_jobs[i]):
                    par = (4 * i + j) % 2
                    sQ = sQ0 if par == 0 else sQ1
                    sPO = sPO0 if par == 0 else sPO1
                    f32v = yq[:, par * 2048: par * 2048 + 2048]
                    f8v = yq8[:, 16384 + par * 2048: 16384 + par * 2048 + 2048]
                    pg, pspv = qstate["use"][par]
                    wait("sync", VE, pg)
                    lv = dma("sync", f32v, src, sQ)
                    wait("vector", sQ, lv)
                    wait("vector", sPO, pspv)
                    gv = run("vector", lambda e, o=f8v, s=f32v: e.tensor_copy(o, s), VE)
                    wait("scalar", VE, gv)
                    sv = dma("scalar", dst,
                             f8v.rearrange("p (u h) -> p u h", u=4), sPO)
                    qstate["use"][par] = (gv, sv)
            return

        def q_done_val():
            return (sPO0.n, sPO1.n) if qstate["i"] >= len(q_jobs) else None

        st = {"sparity": 0, "yfree": 0, "zfree": 0, "stagefree_v": 0,
              "stagefree_g": 0, "stagefree_a": 0, "stagefree_t": 0,
              "uT_cons": 0, "xin_free_v": v_init, "xin_free_t": 0,
              "wfree_v": v_init}

        def emit_ln_uT(xsrc, blk):
            for tt in range(2):
                xin = xin1
                wait("sync", VE, st["xin_free_v"])
                wait("sync", AC, st.get("xin_free_a", 0))
                wait("sync", GP, st.get("y_done", 0))
                lv = dma("sync", xin, xsrc[blk * TB + tt * 128: blk * TB + (tt + 1) * 128, :], sX)
                wait("vector", sX, lv)
                mcol = colsb[:, 8 * tt + 0: 8 * tt + 1]
                vcol = colsb[:, 8 * tt + 1: 8 * tt + 2]
                rstd = colsb[:, 8 * tt + 2: 8 * tt + 3]
                stok = colsb[:, 8 * tt + 3: 8 * tt + 4]
                tmp = colsb[:, 8 * tt + 4: 8 * tt + 5]
                wait("vector", AC, st.get("colsb_ac", 0))
                rv = run("vector", lambda e, x=xin, o=mcol: e.reduce_sum(o, x, axis=AXX), VE)
                wait("vector", VE, rv)
                mv0 = run("vector", lambda e, o=mcol: e.tensor_scalar_mul(o, o, 1.0 / D), VE)
                wait("vector", VE, mv0)
                wait("vector", TE, st["uT_cons"])
                run("vector", lambda e, x=xin, m=mcol: e.tensor_scalar(
                    out=ar_u[:, :], in0=x, scalar1=m, scalar2=None, op0=ALU.subtract), VE)
                wait("vector", sOUT, st.get("x1_free", 0))
                run("vector", lambda e, v=vcol: e.scalar_tensor_tensor(
                    out=lnscr, in0=ar_u[:, :], scalar=1.0, in1=ar_u[:, :],
                    op0=ALU.mult, op1=ALU.mult, accum_out=v), VE)
                wait("vector", VE, VE.n)
                vv = run("vector", lambda e, v=vcol: e.tensor_scalar_mul(v, v, 1.0 / D), VE)
                wait("scalar", VE, vv)
                run("scalar", lambda e, v=vcol, t=tmp: e.activation(t, v, AF.Ln, bias=cEPS), AC)
                av = run("scalar", lambda e, t=tmp, r=rstd: e.activation(r, t, AF.Exp, scale=-0.5), AC)
                wait("vector", AC, av)
                run("vector", lambda e, v=vcol, t=tmp: e.tensor_scalar(
                    out=t, in0=v, scalar1=float(EPS), scalar2=None, op0=ALU.add), VE)
                run("vector", lambda e, t=tmp: e.reciprocal(t, t), VE)
                vv = run("vector", lambda e, v=vcol, t=tmp: e.tensor_tensor(t, v, t, ALU.mult), VE)
                wait("scalar", VE, vv)
                run("scalar", lambda e, t=tmp: e.activation(t, t, AF.Ln, bias=cEPS), AC)
                av = run("scalar", lambda e, t=tmp: e.activation(t, t, AF.Exp, scale=-0.5), AC)
                wait("vector", AC, av)
                sv0 = run("vector", lambda e, r=rstd, t=tmp, s=stok: e.tensor_tensor(s, r, t, ALU.mult), VE)
                wait("vector", VE, sv0)
                uv = run("vector", lambda e, s=stok: e.tensor_scalar(
                    out=ar_u[:, :], in0=ar_u[:, :], scalar1=s, scalar2=None, op0=ALU.mult), VE)
                st["xin_free_v"] = uv
                wait("tensor", VE, uv)
                wait("vector", TE, st.get("ut_free", 0))
                for k in range(NKT):
                    bk = banks[6 + (k % 2)]
                    bk.begin_write()
                    tv = run("tensor", lambda e, k=k, b=bk: e.transpose(
                        b.ap[:, 0:128], ar_u[:, k * 128:(k + 1) * 128], i32sb[:, :]), TE)
                    bk.wrote_group(tv)
                    bk.read("scalar")
                    ev = run("scalar", lambda e, k=k, tt=tt, b=bk: e.activation(
                        ar_ut[:, k * TB + tt * 128: k * TB + tt * 128 + 128],
                        b.ap[:, 0:128], AF.Copy, bias=0.0,
                        scale=rmssb[:, k:k + 1]), AC)
                    bk.consumed(AC, ev)
                st["uT_cons"] = TE.n
            st["uT_ready"] = VE.n
            st["uT_ready_a"] = AC.n

        def emit_inproj_xbcdt(blk, part):
            if part == "head":
                wait("tensor", VE, st["uT_ready"])
                wait("tensor", AC, st.get("uT_ready_a", 0))
                wait("vector", AC, st["stagefree_a"])
                wait("vector", GP, st["stagefree_g"])
                wait("vector", TE, st["stagefree_t"])
                wait("vector", AC, st.get("dt_free", 0))
                wait("vector", GP, st.get("halo_w_g", 0))
                st["halo_v"] = run("vector", lambda e: e.tensor_copy(
                    ar_stage[:, 0:NCT * STW].rearrange("p (c w) -> p c w", c=NCT)[:, :, 0:3],
                    halo3[:, 0:NCT * 3].rearrange("p (c w) -> p c w", c=NCT)), VE)
                seq = [(0, 33)]
            else:
                seq = list(enumerate([32] + list(range(32)), start=1))
            for j, i in seq:
                et = 32 + i
                w = wA if j % 2 == 0 else wB
                ecols = 128 if et < 65 else 64
                qe = "sync" if j % 2 == 0 else "sync"
                wait(qe, TE, st.get(f"wc{j % 2}", 0))
                wait(qe, VE, st["wfree_v"])
                sw = sWA if j % 2 == 0 else sWB
                lv = dma(qe, w[:, 0:NKT * ecols],
                         dWT[:, et * 2048: et * 2048 + NKT * ecols], sw)
                bk = banks[j % 4]
                pa = bk.ap[:, 0:256]
                wait("tensor", sw, lv)
                bk.begin_write()
                for k in range(NKT):
                    is_last = (k == NKT - 1)
                    sem = TE if is_last else None
                    run("tensor", lambda e, k=k, w=w, pa=pa, ec=ecols: e.matmul(
                        pa[0:ec, :] if ec < 128 else pa,
                        w[:, k * ec:(k + 1) * ec],
                        ar_ut[:, k * TB:(k + 1) * TB],
                        start=(k == 0), stop=(k == NKT - 1)), sem)
                tv = TE.n
                st[f"wc{j % 2}"] = tv
                bk.wrote_group(tv)
                bk.read("scalar")
                if et < 65:
                    ev = run("scalar", lambda e, i=i, pa=pa: e.copy(
                        ar_stage[:, i * STW + 3: i * STW + 3 + TB], pa), AC)
                    st[f"stev_{i}"] = ev
                else:
                    ev = run("scalar", lambda e, pa=pa: e.copy(
                        dtraw[:, :], pa[0:64, :]), AC)
                    st["dtraw_v"] = ev
                bk.consumed(AC, ev)
            if part == "rest":
                st["stage_ready"] = AC.n

        def emit_conv(blk):
            gpj = 0
            for j, i in enumerate([32] + list(range(32))):
                on_gp = (j % 2 == 1)
                eng = "gpsimd" if on_gp else "vector"
                sem = GP if on_gp else VE
                base = i * STW
                wait(eng, AC, st.get(f"stev_{i}", 0))
                wait(eng, VE, st["halo_v"])
                if on_gp:
                    aa, ag = (acc1, accg) if gpj % 2 == 0 else (acc2, accg2)
                    gpj += 1
                    wait(eng, AC, st.get(f"acc_cons_gp{gpj % 2}", 0))
                    run(eng, lambda e, base=base, i=i, aa=aa: e.tensor_tensor(
                        aa[:, :], ar_stage[:, base: base + TB],
                        cwsb[:, 4 * i: 4 * i + 1].broadcast_to([128, TB]),
                        ALU.mult), sem)
                    for kk in range(1, 4):
                        run(eng, lambda e, base=base, kk=kk, i=i, ag=ag: e.tensor_tensor(
                            ag[:, :], ar_stage[:, base + kk: base + kk + TB],
                            cwsb[:, 4 * i + kk: 4 * i + kk + 1].broadcast_to([128, TB]),
                            ALU.mult), sem)
                        run(eng, lambda e, aa=aa, ag=ag: e.tensor_tensor(
                            aa[:, :], aa[:, :], ag[:, :], ALU.add), sem)
                    accv = aa
                    acck = f"acc_cons_gp{gpj % 2}"
                else:
                    wait(eng, AC, st.get("acc_cons_ve", 0))
                    run(eng, lambda e, base=base, i=i: e.tensor_scalar(
                        out=acc0[:, :], in0=ar_stage[:, base: base + TB],
                        scalar1=cwsb[:, 4 * i: 4 * i + 1], scalar2=None, op0=ALU.mult), sem)
                    for kk in range(1, 4):
                        run(eng, lambda e, base=base, kk=kk, i=i: e.scalar_tensor_tensor(
                            out=acc0[:, :], in0=ar_stage[:, base + kk: base + kk + TB],
                            scalar=cwsb[:, 4 * i + kk: 4 * i + kk + 1], in1=acc0[:, :],
                            op0=ALU.mult, op1=ALU.add), sem)
                    accv = acc0
                    acck = "acc_cons_ve"
                cv = run(eng, lambda e, base=base, i=i: e.tensor_copy(
                    halo3[:, i * 3: i * 3 + 3], ar_stage[:, base + TB: base + TB + 3]), sem)
                wait("scalar", sem, cv)
                run("scalar", lambda e, base=base, a=accv, i=i: e.activation(
                    ar_stage[:, base + 3: base + 3 + TB], a[:, :], AF.Silu,
                    bias=cbsb[:, i:i + 1]), AC)
                st[acck] = AC.n
                st[f"convt_{i}"] = AC.n
            st["conv_done"] = AC.n
            st["halo_w_g"] = GP.n

        def emit_dt_head(blk):
            wait("scalar", AC, st["dtraw_v"])
            wait("scalar", TE, st.get("scan_te", 0))
            wait("vector", TE, st.get("scan_te", 0))
            wait("vector", AC, st.get("scan_ac", 0))
            run("scalar", lambda e: e.activation(dtval[:, :], dtraw[:, :], AF.Exp,
                                                 bias=dtbsb[:, :]), AC)
            run("scalar", lambda e: e.activation(dtval[:, :], dtval[:, :], AF.Ln, bias=cONE[0:64, :]), AC)
            av = run("scalar", lambda e: e.activation(lndt[:, :], dtval[:, :], AF.Ln), AC)
            wait("vector", AC, av)
            run("vector", lambda e: e.tensor_scalar(out=dta[:, :], in0=dtval[:, :],
                                                    scalar1=acolsb[:, :], scalar2=None,
                                                    op0=ALU.mult), VE)
            wait("vector", sLA, st.get("las_h", 0))
            for c in range(NCH):
                run("vector", lambda e, c=c: e.tensor_tensor_scan(
                    out=lat[:, c * CH:(c + 1) * CH], data0=onessb[0:64, 0:CH],
                    data1=dta[:, c * CH:(c + 1) * CH], initial=0.0,
                    op0=ALU.mult, op1=ALU.add), VE)
            lav = VE.n
            st["lav"] = lav

        def emit_dt_tail(blk, with_wcols):
            lav = st["lav"]
            wait("sync", VE, lav)
            wait("sync", sLA, st.get("sla_last", 0))
            st["sla_last"] = dma("sync", dlas[:, :], lat[:, :], sLA)
            st["las_h"] = st["sla_last"]
            wait("tensor", VE, lav)
            bk = banks[7]
            for c in range(NCH):
                for srcap, dst in ((lat[:, c * CH:(c + 1) * CH], laT),
                                   (lndt[:, c * CH:(c + 1) * CH], lndtT)):
                    bk.begin_write()
                    tv = run("tensor", lambda e, s=srcap, b=bk: e.transpose(
                        b.ap[0:CH, 0:64], s, i32sb[0:64, 0:64]), TE)
                    bk.wrote_group(tv)
                    bk.read("vector")
                    ev = run("vector", lambda e, c=c, d=dst, b=bk: e.tensor_copy(
                        d[:, c * 64:(c + 1) * 64], b.ap[0:CH, 0:64]), VE)
                    bk.consumed(VE, ev)
                run("vector", lambda e, c=c: e.tensor_sub(
                    biascol[:, c * 64:(c + 1) * 64], lndtT[:, c * 64:(c + 1) * 64],
                    laT[:, c * 64:(c + 1) * 64]), VE)
                bv = VE.n
                wait("scalar", VE, bv)
                run("scalar", lambda e, c=c: e.activation(
                    elatok[:, c * 64:(c + 1) * 64], laT[:, c * 64:(c + 1) * 64], AF.Exp), AC)
            wait("vector", VE, VE.n)
            st["dt_done_a"] = AC.n
            st["dt_done_v"] = VE.n
            st["dt_free"] = AC.n
            bk = banks[7]
            for c in range(NCH):
                wait("vector", TE, st.get("lcb_cons", 0))
                lv2 = run("vector", lambda e, c=c: e.tensor_scalar(
                    out=lcbsb[:, :], in0=onessb[0:64, 0:128],
                    scalar1=lat[:, c * CH + 127: c * CH + 128],
                    scalar2=None, op0=ALU.mult), VE)
                wait("tensor", VE, lv2)
                bk.begin_write()
                tv = run("tensor", lambda e, b=bk: e.matmul(
                    b.ap[:, 384:448], lcbsb[:, :], i32sb[0:64, 0:64],
                    start=True, stop=True), TE)
                st["lcb_cons"] = tv
                bk.wrote_group(tv)
                bk.read("scalar")
                av = run("scalar", lambda e, c=c, b=bk: e.activation(
                    elac[:, c * 64:(c + 1) * 64], b.ap[0:64, 384:448], AF.Exp), AC)
                bk.consumed(AC, av)
                if with_wcols:
                    bk.read("vector")
                    wait("vector", AC, av)
                    ev = run("vector", lambda e, c=c, b=bk: e.tensor_add(
                        wcols[:, c * 64:(c + 1) * 64], biascol[:, c * 64:(c + 1) * 64],
                        b.ap[:, 384:448]), VE)
                    bk.consumed(VE, ev)
                    wait("scalar", VE, ev)
                    run("scalar", lambda e, c=c: e.activation(
                        wcols[:, c * 64:(c + 1) * 64], wcols[:, c * 64:(c + 1) * 64],
                        AF.Exp), AC)
            st["cols_done"] = AC.n

        def emit_bct(blk, c, with_c0):
            bcbase = 32 * STW + 3 + c * CH
            bk = banks[7]
            wait("tensor", AC, st["convt_32"])
            wait("vector", TE, st.get("scan_te", 0))
            bk.begin_write()
            tv = run("tensor", lambda e, b=bk, o=bcbase: e.transpose(
                b.ap[:, 320:448], ar_stage[:, o:o + 128], i32sb[:, :]), TE)
            bk.wrote_group(tv)
            bk.read("vector")
            ev = run("vector", lambda e, b=bk: e.tensor_copy(bct[:, :], b.ap[:, 320:448]), VE)
            bk.consumed(VE, ev)
            st["bct_v"] = ev
            if with_c0:
                wait("tensor", VE, ev)
                bk.begin_write()
                tv = run("tensor", lambda e, b=bk: e.transpose(
                    b.ap[0:64, 320:448], bct[:, 64:128], i32sb[:, :]), TE)
                bk.wrote_group(tv)
                bk.read("vector")
                ev = run("vector", lambda e, b=bk: e.tensor_copy(
                    c0sb[:, :], b.ap[0:64, 320:448]), VE)
                bk.consumed(VE, ev)
                st["c0_v"] = ev

        def emit_xt_group(blk, c, g):
            xt = xt0 if g % 2 == 0 else xt1
            wait("scalar", TE, st.get(f"xtcons_{g % 2}", 0))
            wait("tensor", AC, max(st[f"convt_{g * 2}"], st[f"convt_{g * 2 + 1}"]))
            bk = banks[3]
            for q in range(2):
                ct = g * 2 + q
                base = ct * STW + 3 + c * CH
                bk.begin_write()
                tv = run("tensor", lambda e, b=bk, o=base: e.transpose(
                    b.ap[:, 0:128], ar_stage[:, o:o + 128], i32sb[:, :]), TE)
                bk.wrote_group(tv)
                bk.read("scalar")
                ev = run("scalar", lambda e, b=bk, q=q, xt=xt: e.copy(
                    xt[:, q * 128:(q + 1) * 128], b.ap[:, 0:128]), AC)
                bk.consumed(AC, ev)
            return xt, AC.n

        def xslice(xt, hh):
            o = (hh // 2) * 128 + (hh % 2) * 64
            return xt[:, o:o + 64]

        def emit_scan_chunk(blk, c, phase):
            main = phase == "main"
            emit_bct(blk, c, with_c0=main)
            sprev = s_a if st["sparity"] % 2 == 0 else s_b
            snext = s_b if st["sparity"] % 2 == 0 else s_a
            st["sparity"] += 1
            if main:
                gbk = banks[3]
                bcbase = 32 * STW + 3 + c * CH
                wait("tensor", VE, st["c0_v"])
                gbk.begin_write()
                gv = run("tensor", lambda e, b=gbk, o=bcbase: e.matmul(
                    b.ap[:, 0:128], ar_stage[0:64, o:o + 128], c0sb[:, :],
                    start=True, stop=True), TE)
                gbk.wrote_group(gv)
                gbk.read("vector")
                gev = run("vector", lambda e, b=gbk: e.tensor_copy(g_sb[:, :], b.ap[:, 0:128]), VE)
                gbk.consumed(VE, gev)
                st["g_sb_v"] = gev
            def emit_p4_mask(g, c=c):
                if g % 4 == 0:
                    q = g // 4
                    p4buf = p4A if q % 2 == 0 else p4B
                    wait("sync", sLA, st["sla_last"])
                    wait("sync", GP, st.get(f"p4c_{q % 2}", 0))
                    st["sla_last"] = dma(
                        "sync", p4buf[:, :].rearrange("p (h t) -> p h t", h=16),
                        dlas[q * 16:(q + 1) * 16, c * CH:(c + 1) * CH]
                        .rearrange("(o h) t -> o h t", o=1)
                        .broadcast_to([128, 16, 128]), sLA)
                    st[f"p4v_{q % 2}"] = st["sla_last"]
                p4buf = p4A if (g // 4) % 2 == 0 else p4B
                pm = p4m_a if g % 2 == 0 else p4m_b
                wait("gpsimd", sLA, st[f"p4v_{(g // 4) % 2}"])
                wait("gpsimd", AC, st.get(f"pmc_{g % 2}", 0))
                gv = run("gpsimd", lambda e, g=g, pm=pm, pb=p4buf: e.tensor_tensor(
                    pm[:, :], pb[:, (g % 4) * 512:(g % 4 + 1) * 512],
                    sunegsb[:, :], ALU.add), GP)
                st[f"pm_{g % 2}"] = gv
                if g % 4 == 3:
                    st[f"p4c_{(g // 4) % 2}"] = gv

            for g in range(16):
                xt, xtv = emit_xt_group(blk, c, g)
                dsb = banks[4 + (g % 2)]
                wait("tensor", AC, st["cols_done"])
                wait("tensor", VE, st["bct_v"])
                if main:
                    if g == 0:
                        emit_p4_mask(0)
                    ybk = banks[0 + (g % 2)]
                    cbk = banks[2 if g % 2 == 0 else 6]
                    cbk.begin_write()
                    for hh in range(4):
                        h = g * 4 + hh
                        run("tensor", lambda e, hh=hh, h=h, b=cbk, sp=sprev: e.matmul(
                            b.ap[:, hh * 64:(hh + 1) * 64], c0sb[:, :],
                            sp[:, h * 64:(h + 1) * 64],
                            start=(hh == 0), stop=(hh == 3)), TE if hh == 3 else None)
                    cbk.wrote_group(TE.n)
                    ybk.begin_write()
                    pm = p4m_a if g % 2 == 0 else p4m_b
                dsb.begin_write()
                for hh in range(4):
                    h = g * 4 + hh
                    par = hh % 2
                    ebf = (e_a, e_b)[par]
                    mbf = (m_a, m_b)[par]
                    bwb = bwsb if par == 0 else bwsb2
                    if main:
                        wait("scalar", GP, st[f"pm_{g % 2}"])
                        wait("scalar", GP, st.get(f"econs_g_{par}", 0))
                        wait("scalar", VE, st.get(f"econs_v_{par}", 0))
                        av = run("scalar", lambda e, h=h, c=c, hh=hh, ef=ebf, pm=pm: e.activation(
                            ef[:, :], pm[:, hh * 128:(hh + 1) * 128], AF.Exp,
                            bias=biascol[:, c * 64 + h: c * 64 + h + 1]), AC)
                        wait("gpsimd", AC, av)
                        wait("gpsimd", TE, st.get(f"mcons_{par}", 0))
                        wait("gpsimd", VE, st["g_sb_v"])
                        mv = run("gpsimd", lambda e, ef=ebf, mf=mbf: e.tensor_tensor(
                            mf[:, :], ef[:, :], g_sb[:, :], ALU.mult), GP)
                        st[f"econs_g_{par}"] = mv
                        st[f"pmc_{g % 2}"] = av
                        wait("vector", AC, av)
                        wait("vector", TE, st.get(f"bwc_{par}", 0))
                        bv = run("vector", lambda e, ef=ebf, bw=bwb: e.tensor_scalar(
                            out=bw[:, :], in0=bct[:, 0:64],
                            scalar1=ef[:, 127:128],
                            scalar2=None, op0=ALU.mult), VE)
                        st[f"econs_v_{par}"] = bv
                    else:
                        wait("vector", AC, st["cols_done"])
                        wait("vector", TE, st.get(f"bwc_{par}", 0))
                        bv = run("vector", lambda e, h=h, c=c, bw=bwb: e.tensor_scalar(
                            out=bw[:, :], in0=bct[:, 0:64],
                            scalar1=wcols[:, c * 64 + h: c * 64 + h + 1],
                            scalar2=None, op0=ALU.mult), VE)
                    if main:
                        wait("tensor", GP, mv)
                        wait("tensor", AC, xtv)
                        iv = run("tensor", lambda e, hh=hh, mf=mbf, b=ybk, xt=xt: e.matmul(
                            b.ap[:, hh * 64:(hh + 1) * 64], mf[:, :], xslice(xt, hh),
                            start=(hh == 0), stop=(hh == 3)), TE)
                        st[f"mcons_{par}"] = iv
                        ybk.wrote_group(iv)
                    wait("tensor", VE, bv)
                    wait("tensor", AC, xtv)
                    dv = run("tensor", lambda e, hh=hh, b=dsb, xt=xt, bw=bwb: e.matmul(
                        b.ap[0:64, hh * 64:(hh + 1) * 64], bw[:, :], xslice(xt, hh),
                        start=True, stop=True), TE)
                    dsb.wrote_group(dv)
                    st[f"bwc_{par}"] = dv
                if main:
                    if g < 15:
                        emit_p4_mask(g + 1)
                    ybk.read("vector")
                    cbk.read("scalar")
                    wait("vector", AC, st["z_done"])
                    wait("gpsimd", AC, st["z_done"])
                    for hh in range(4):
                        h = g * 4 + hh
                        par = hh % 2
                        ysl = ar_y[:, c * DI + h * 64: c * DI + (h + 1) * 64]
                        ysc = yscr0 if par == 0 else yscr1
                        ycb = ycr0 if par == 0 else ycr1
                        wait("vector", GP, st.get(f"yscons_{par}", 0))
                        run("vector", lambda e, hh=hh, h=h, xt=xt, ysc=ysc, b=ybk: e.scalar_tensor_tensor(
                            out=ysc[:, :], in0=xslice(xt, hh), scalar=drepsb[:, h:h + 1],
                            in1=b.ap[:, hh * 64:(hh + 1) * 64],
                            op0=ALU.mult, op1=ALU.add), VE)
                        wait("scalar", VE, st.get(f"ycbc_{par}", 0))
                        av = run("scalar", lambda e, hh=hh, h=h, c=c, b=cbk, ycb=ycb: e.activation(
                            ycb[:, :], b.ap[:, hh * 64:(hh + 1) * 64],
                            AF.Copy, bias=0.0,
                            scale=elatok[:, c * 64 + h: c * 64 + h + 1]), AC)
                        wait("vector", AC, av)
                        vv = run("vector", lambda e, ysc=ysc, ycb=ycb: e.tensor_tensor(
                            ysc[:, :], ysc[:, :], ycb[:, :], ALU.add), VE)
                        st[f"ycbc_{par}"] = vv
                        wait("gpsimd", VE, vv)
                        run("gpsimd", lambda e, ysl=ysl, ysc=ysc: e.tensor_tensor(
                            ysl, ysc[:, :], ysl, ALU.mult), GP)
                        st[f"yscons_{par}"] = GP.n
                    ybk.consumed(VE, VE.n)
                    cbk.consumed(AC, AC.n)
                dsb.read("vector")
                wait("vector", AC, st["cols_done"])
                for hh in range(4):
                    h = g * 4 + hh
                    run("vector", lambda e, hh=hh, h=h, c=c, b=dsb, sp=sprev, sn=snext: e.scalar_tensor_tensor(
                        out=sn[:, h * 64:(h + 1) * 64], in0=sp[:, h * 64:(h + 1) * 64],
                        scalar=elac[:, c * 64 + h: c * 64 + h + 1],
                        in1=b.ap[0:64, hh * 64:(hh + 1) * 64],
                        op0=ALU.mult, op1=ALU.add), VE)
                dsb.consumed(VE, VE.n)
                st[f"xtcons_{g % 2}"] = TE.n
            if main:
                st["y_done"] = GP.n
            st["scan_te"] = TE.n
            st["scan_ve"] = VE.n
            st["scan_ac"] = AC.n

        def emit_z_early(blk):
            wait("scalar", VE, st["uT_ready"])
            wait("scalar", AC, st.get("uT_ready_a", 0)) if False else None
            wait("scalar", TE, st.get("ynt_te", 0))
            wait("tensor", AC, st.get("uT_ready_a", 0))
            for zt in range(NXT):
                w = wA if zt % 2 == 0 else wB
                qe = "sync" if zt % 2 == 0 else "sync"
                wait(qe, TE, st.get(f"wc{zt % 2}", 0))
                sw = sWA if zt % 2 == 0 else sWB
                lv = dma(qe, w[:, 0:NKT * 128],
                         dWT[:, zt * 2048:(zt + 1) * 2048], sw)
                bk = banks[4 + (zt % 2)]
                wait("tensor", sw, lv)
                bk.begin_write()
                for tt in range(2):
                    for k in range(NKT):
                        is_last = (tt == 1 and k == NKT - 1)
                        run("tensor", lambda e, k=k, tt=tt, w=w, b=bk: e.matmul(
                            b.ap[:, tt * 128:(tt + 1) * 128],
                            ar_ut[:, k * TB + tt * 128: k * TB + tt * 128 + 128],
                            w[:, k * 128:(k + 1) * 128],
                            start=(k == 0), stop=(k == NKT - 1)), TE if is_last else None)
                tv = TE.n
                st[f"wc{zt % 2}"] = tv
                bk.wrote_group(tv)
                bk.read("scalar")
                av = None
                for c in range(NCH):
                    av = run("scalar", lambda e, b=bk, c=c, zt=zt: e.activation(
                        ar_y[:, c * DI + zt * 128: c * DI + zt * 128 + 128],
                        b.ap[:, c * 128:(c + 1) * 128], AF.Silu), AC)
                bk.consumed(AC, av)
            st["z_done"] = AC.n

        def emit_gated_ynt(blk):
            wait("vector", GP, st["y_done"])
            wait("vector", TE, st["scan_te"])
            wait("vector", GP, st["halo_w_g"])
            wait("vector", AC, st["conv_done"])
            for c in range(NCH):
                for half in range(2):
                    run("vector", lambda e, c=c, half=half: e.scalar_tensor_tensor(
                        out=lnscr, in0=ar_y[:, c * DI + half * D: c * DI + (half + 1) * D],
                        scalar=1.0,
                        in1=ar_y[:, c * DI + half * D: c * DI + (half + 1) * D],
                        op0=ALU.mult, op1=ALU.mult,
                        accum_out=ssqcol[:, 2 * c + half: 2 * c + half + 1]), VE)
                wait("vector", VE, VE.n)
                vv = run("vector", lambda e, c=c: e.tensor_add(
                    ssqcol[:, 2 * c: 2 * c + 1], ssqcol[:, 2 * c: 2 * c + 1],
                    ssqcol[:, 2 * c + 1: 2 * c + 2]), VE)
                wait("scalar", VE, vv)
                run("scalar", lambda e, c=c: e.activation(
                    gcol[:, c:c + 1], ssqcol[:, 2 * c: 2 * c + 1], AF.Ln,
                    bias=cEPS, scale=1.0 / DI), AC)
                av = run("scalar", lambda e, c=c: e.activation(
                    gcol[:, c:c + 1], gcol[:, c:c + 1], AF.Exp, scale=-0.5), AC)
                wait("vector", AC, av)
                ynv = run("vector", lambda e, c=c: e.tensor_scalar(
                    out=ar_y[:, c * DI:(c + 1) * DI], in0=ar_y[:, c * DI:(c + 1) * DI],
                    scalar1=gcol[:, c:c + 1], scalar2=None, op0=ALU.mult), VE)
                wait("tensor", VE, ynv)
                for et in range(NXT):
                    bk = banks[4 + (et % 2)]
                    bk.begin_write()
                    tv = run("tensor", lambda e, c=c, et=et, b=bk: e.transpose(
                        b.ap[:, 0:128], ar_y[:, c * DI + et * 128: c * DI + (et + 1) * 128],
                        i32sb[:, :]), TE)
                    bk.wrote_group(tv)
                    bk.read("scalar")
                    ev = run("scalar", lambda e, c=c, et=et, b=bk: e.activation(
                        ynt_ap[:, et * TB + c * 128: et * TB + c * 128 + 128],
                        b.ap[:, 0:128], AF.Copy, bias=0.0,
                        scale=normsb[:, et:et + 1]), AC)
                    bk.consumed(AC, ev)
            st["ynt_done_a"] = AC.n
            st["ynt_te"] = TE.n
            st["ynt_done"] = VE.n

        def emit_outproj_x1(blk):
            r0 = ar_y.ap()[:, 0:2 * D]
            xin2 = ar_y.ap()[:, 2 * D:4 * D]
            wait("sync", VE, st["ynt_done"])
            wait("sync", TE, st["ynt_te"])
            for tt in range(2):
                sx2 = sX2A if tt == 0 else sX2B
                xv = dma("sync", xin2[:, tt * D:(tt + 1) * D],
                         dx[blk * TB + tt * 128: blk * TB + (tt + 1) * 128, :], sx2)
                mcol = colsb[:, 8 * tt + 0: 8 * tt + 1]
                rstd = colsb[:, 8 * tt + 2: 8 * tt + 3]
                wait("vector", sx2, xv)
                wait("vector", TE, st["ynt_te"])
                wait("vector", TE, st["uT_cons"])
                run("vector", lambda e, tt=tt, m=mcol, r=rstd, x2=xin2: e.tensor_scalar(
                    out=ar_u[:, :], in0=x2[:, tt * D:(tt + 1) * D],
                    scalar1=m, scalar2=r, op0=ALU.subtract, op1=ALU.mult), VE)
                run("vector", lambda e, tt=tt, x2=xin2, r0=r0: e.tensor_add(
                    r0[:, tt * D:(tt + 1) * D], x2[:, tt * D:(tt + 1) * D], ar_u[:, :]), VE)
            r0v = VE.n
            st["xin_free_v"] = r0v
            wait("tensor", AC, st["ynt_done_a"])
            for dmb in range(4):
                bk0 = banks[0]
                bk1 = banks[1]
                bk0.begin_write()
                bk1.begin_write()
                for kg in range(8):
                    w = wA if kg % 2 == 0 else wB
                    sw = sWA if kg % 2 == 0 else sWB
                    qe = "sync" if kg % 2 == 0 else "sync"
                    wait(qe, TE, st.get(f"wc{kg % 2}", 0))
                    lv = dma(qe, w[:, 0:2048],
                             dWoT[:, (dmb * 8 + kg) * 2048:(dmb * 8 + kg + 1) * 2048], sw)
                    wait("tensor", sw, lv)
                    for kk in range(4):
                        k = kg * 4 + kk
                        for tt, bk in ((0, bk0), (1, bk1)):
                            last_of_group = (kk == 3 and tt == 1)
                            run("tensor", lambda e, k=k, kk=kk, tt=tt, w=w, b=bk: e.matmul(
                                b.ap[:, 0:512],
                                ynt_ap[:, k * TB + tt * 128: k * TB + tt * 128 + 128],
                                w[:, kk * 512:(kk + 1) * 512],
                                start=(k == 0), stop=(k == 31)),
                                TE if last_of_group else None)
                    st[f"wc{kg % 2}"] = TE.n
                tv = TE.n
                bk0.wrote_group(tv)
                bk1.wrote_group(tv)
                for tt, bk in ((0, bk0), (1, bk1)):
                    bk.read("vector")
                    wait("vector", VE, r0v)
                    wait("vector", sOUT, st.get("x1_free", 0))
                    ev = run("vector", lambda e, tt=tt, dmb=dmb, b=bk, r0=r0: e.tensor_add(
                        ar_x1[:, tt * D + dmb * 512: tt * D + (dmb + 1) * 512],
                        r0[:, tt * D + dmb * 512: tt * D + (dmb + 1) * 512],
                        b.ap[:, 0:512]), VE)
                    bk.consumed(VE, ev)
            st["x1_done"] = VE.n
            st["opj_te"] = TE.n

        def emit_ffn_out(blk):
            qd = q_done_val()
            if qd:
                wait("sync", sPO0, qd[0])
                wait("sync", sPO1, qd[1])
            aq = aqT()
            xq = xqT()
            wait("vector", TE, st["opj_te"])
            for tt in range(2):
                x1t = ar_x1[:, tt * D:(tt + 1) * D]
                mcol = colsb[:, 8 * tt + 0: 8 * tt + 1]
                vcol = colsb[:, 8 * tt + 1: 8 * tt + 2]
                rstd = colsb[:, 8 * tt + 2: 8 * tt + 3]
                wait("vector", VE, st["x1_done"])
                rv = run("vector", lambda e, x=x1t, o=mcol: e.reduce_sum(o, x, axis=AXX), VE)
                wait("vector", VE, rv)
                mv0 = run("vector", lambda e, o=mcol: e.tensor_scalar_mul(o, o, 1.0 / D), VE)
                wait("vector", VE, mv0)
                wait("vector", TE, st.get("lnu_cons", 0))
                wait("vector", AC, st.get("lnu_cons_a", 0))
                run("vector", lambda e, x=x1t, m=mcol: e.tensor_scalar(
                    out=ar_u[:, :], in0=x, scalar1=m, scalar2=None, op0=ALU.subtract), VE)
                vv = run("vector", lambda e, v=vcol: e.scalar_tensor_tensor(
                    out=ln2scr, in0=ar_u[:, :], scalar=1.0, in1=ar_u[:, :],
                    op0=ALU.mult, op1=ALU.mult, accum_out=v), VE)
                wait("scalar", VE, vv)
                run("scalar", lambda e, v=vcol, r=rstd: e.activation(
                    r, v, AF.Ln, bias=cEPS, scale=1.0 / D), AC)
                av = run("scalar", lambda e, r=rstd: e.activation(r, r, AF.Exp, scale=-0.5), AC)
                st["colsb_ac"] = av
                wait("vector", AC, av)
                lnv = run("vector", lambda e, r=rstd: e.tensor_scalar(
                    out=ar_u[:, :], in0=ar_u[:, :], scalar1=r, scalar2=None,
                    op0=ALU.mult), VE)
                wait("tensor", VE, lnv)
                for k in range(NKT):
                    bk = banks[6 + (k % 2)]
                    bk.begin_write()
                    tv = run("tensor", lambda e, k=k, b=bk: e.transpose(
                        b.ap[:, 0:128], ar_u[:, k * 128:(k + 1) * 128], i32sb[:, :]), TE)
                    bk.wrote_group(tv)
                    bk.read("vector")
                    ev = run("vector", lambda e, k=k, tt=tt, b=bk, xq=xq: e.tensor_copy(
                        xq[:, k * TB + tt * 128: k * TB + tt * 128 + 128], b.ap[:, 0:128]), VE)
                    bk.consumed(VE, ev)
                st["lnu_cons"] = TE.n
            xqv = VE.n
            wait("tensor", VE, xqv)
            w8A = wA.ap().bitcast(FP8)
            w8B = wB.ap().bitcast(FP8)

            def emit_aq_transposes(pend):
                pv, hb, tt = pend
                wait("tensor", VE, pv)
                for q in range(4):
                    k2 = hb * 4 + q
                    bk = banks[4 + (q % 2)]
                    bk.begin_write()
                    pb = bk.ap.bitcast(BF16)
                    tv = run("tensor", lambda e, q=q, pb=pb: e.transpose(
                        pb[:, 0:128], prodsb[:, q * 128:(q + 1) * 128], ibfsb[:, :]), TE)
                    bk.wrote_group(tv)
                    bk.read("vector")
                    ev = run("vector", lambda e, k2=k2, tt=tt, pb=pb, aq=aq: e.tensor_copy(
                        aq[:, k2 * TB + tt * 128: k2 * TB + tt * 128 + 128],
                        pb[:, 0:128]), VE)
                    bk.consumed(VE, ev)
                st["prod_cons"] = TE.n

            w8C = ar_ut.ap().bitcast(FP8)[:, 0:8192]
            w8D = ar_ut.ap().bitcast(FP8)[:, 8192:16384]
            pending = None
            for hb in range(16):
                if hb % 2 == 0:
                    wg8, wu8, sg, su = w8A, w8B, sWA, sWB
                    kgk, kuk = "wc0", "wc1"
                else:
                    wg8, wu8, sg, su = w8C, w8D, sWC, sWD
                    kgk, kuk = "wcC", "wcD"
                if hb == 1:
                    wait("sync", TE, st["ut_free"])
                wait("sync", TE, st.get(kgk, 0))
                lv1 = dma("sync", wg8[:, 0:NKT * 512],
                          dg0q[:, hb * 8192:(hb + 1) * 8192], sg)
                wait("sync", TE, st.get(kuk, 0))
                lv2 = dma("sync", wu8[:, 0:NKT * 512],
                          dg1q[:, hb * 8192:(hb + 1) * 8192], su)
                wait("tensor", sg, lv1)
                wait("tensor", su, lv2)
                for tt in range(2):
                    p2 = (hb * 2 + tt) % 2
                    gbk = banks[0 + p2]
                    ubk = banks[2 + p2]
                    gbk.begin_write()
                    ubk.begin_write()
                    for k in range(NKT):
                        run("tensor", lambda e, k=k, tt=tt, b=gbk, w=wg8, xq=xq: e.matmul(
                            b.ap[:, 0:512],
                            xq[:, k * TB + tt * 128: k * TB + tt * 128 + 128],
                            w[:, k * 512:(k + 1) * 512],
                            start=(k == 0), stop=(k == NKT - 1)), None)
                    for k in range(NKT):
                        is_last = k == NKT - 1
                        run("tensor", lambda e, k=k, tt=tt, b=ubk, w=wu8, xq=xq: e.matmul(
                            b.ap[:, 0:512],
                            xq[:, k * TB + tt * 128: k * TB + tt * 128 + 128],
                            w[:, k * 512:(k + 1) * 512],
                            start=(k == 0), stop=(k == NKT - 1)), TE if is_last else None)
                    tv = TE.n
                    if tt == 1:
                        st[kgk] = tv
                        st[kuk] = tv
                    gbk.wrote_group(tv)
                    ubk.wrote_group(tv)
                    if pending is not None:
                        emit_aq_transposes(pending)
                    gbk.read("vector")
                    ubk.read("vector")
                    run("vector", lambda e, b=gbk: e.tensor_copy(gatesb[:, :], b.ap[:, 0:512]), VE)
                    vv = run("vector", lambda e, b=ubk: e.tensor_copy(upsb[:, :], b.ap[:, 0:512]), VE)
                    gbk.consumed(VE, vv)
                    ubk.consumed(VE, vv)
                    wait("scalar", VE, vv)
                    av = run("scalar", lambda e: e.activation(
                        gatesb[:, :], gatesb[:, :], AF.Gelu_apprx_tanh), AC)
                    wait("vector", AC, av)
                    wait("vector", TE, st.get("prod_cons", 0))
                    pv = run("vector", lambda e: e.tensor_tensor(
                        prodsb[:, :], gatesb[:, :], upsb[:, :], ALU.mult), VE)
                    pending = (pv, hb, tt)
            emit_aq_transposes(pending)
            st["ut_free"] = TE.n
            aqv = VE.n
            wait("tensor", VE, aqv)
            for tt in range(2):
                for dmb in range(4):
                    banks[tt * 4 + dmb].begin_write()
            for ag in range(16):
                w8 = w8A if ag % 2 == 0 else w8B
                sw = sWA if ag % 2 == 0 else sWB
                qe = "sync" if ag % 2 == 0 else "sync"
                wait(qe, TE, st.get(f"wc{ag % 2}", 0))
                lv = dma(qe, w8[:, 0:8192],
                         dwlq[:, ag * 8192:(ag + 1) * 8192], sw)
                wait("tensor", sw, lv)
                for a in range(4):
                    k2 = ag * 4 + a
                    for tt in range(2):
                        for dmb in range(4):
                            bk = banks[tt * 4 + dmb]
                            is_last = (a == 3 and tt == 1 and dmb == 3)
                            run("tensor", lambda e, k2=k2, a=a, tt=tt, dmb=dmb, w8=w8, b=bk, aq=aq: e.matmul(
                                b.ap[:, 0:512],
                                aq[:, k2 * TB + tt * 128: k2 * TB + tt * 128 + 128],
                                w8[:, a * 2048 + dmb * 512: a * 2048 + (dmb + 1) * 512],
                                start=(k2 == 0), stop=(k2 == 63)),
                                TE if is_last else None)
                st[f"wc{ag % 2}"] = TE.n
            tvf = TE.n
            for tt in range(2):
                for dmb in range(4):
                    bk = banks[tt * 4 + dmb]
                    bk.wrote_group(tvf)
                    bk.read("vector")
                    run("vector", lambda e, b=bk: e.tensor_copy(prodsb[:, :], b.ap[:, 0:512]), VE)
                    ev = run("vector", lambda e, tt=tt, dmb=dmb: e.tensor_tensor(
                        ar_x1[:, tt * D + dmb * 512: tt * D + (dmb + 1) * 512],
                        ar_x1[:, tt * D + dmb * 512: tt * D + (dmb + 1) * 512],
                        prodsb[:, :], ALU.add), VE)
                    bk.consumed(VE, ev)
            x2v = VE.n
            wait("scalar", VE, x2v)
            sv = dma("scalar",
                     dout[blk * TB:(blk + 1) * TB, :].rearrange("(a p) d -> p a d", p=128),
                     ar_x1[:, :].rearrange("p (a d) -> p a d", a=2), sOUT)
            st["x1_free"] = sv

        import os as _os
        _skip_q = bool(int(_os.environ.get("KBISECT_SKIP_Q", "0")))
        if _skip_q:
            qstate["i"] = len(q_jobs)
        _rec_stop = _os.environ.get("KBISECT_REC_STOP", "")
        if not skip_recompute:
            for blk in range(nb):
                emit_q_jobs(12)
                emit_ln_uT(dxp, blk)
                if _rec_stop == "ln":
                    continue
                emit_inproj_xbcdt(blk, "head")
                emit_dt_head(blk)
                emit_inproj_xbcdt(blk, "rest")
                if _rec_stop == "inproj":
                    continue
                emit_dt_tail(blk, with_wcols=True)
                emit_conv(blk)
                if _rec_stop == "conv":
                    continue
                if _rec_stop == "dt":
                    continue
                for c in range(NCH):
                    emit_scan_chunk(blk, c, "recompute")
                st["stagefree_v"] = st["scan_ve"]
                st["stagefree_a"] = st["conv_done"]
                st["stagefree_g"] = st["halo_w_g"]
                st["stagefree_t"] = st["scan_te"]
                st["ut_free"] = TE.n
        else:
            run("vector", lambda e: e.memset(s_b[:, :], 0.0), VE)
            st["sparity"] = 0
        if not _skip_q:
            emit_q_jobs(len(q_jobs))
        sfin = s_a if st["sparity"] % 2 == 0 else s_b
        sini = s_b if st["sparity"] % 2 == 0 else s_a
        wait("vector", TE, TE.n)
        run("vector", lambda e, sf=sfin, si=sini: e.tensor_scalar(
            out=si[:, :], in0=sf[:, :], scalar1=flagsb[:, :], scalar2=None,
            op0=ALU.mult), VE)
        st["sparity"] = 0 if sini is s_a else 1
        tap("s_init", sini[:, :])

        wait("vector", GP, GP.n)
        wait("vector", sPO0, sPO0.n)
        wait("vector", sPO1, sPO1.n)
        wait("scalar", GP, GP.n)
        wait("scalar", sPO0, sPO0.n)
        wait("scalar", sPO1, sPO1.n)

        if True:
            tap("uT", ar_ut[:, :])
            tap("stage", ar_stage[:, 0:NET * STW])
            tap("y", ar_y.ap()[:, 0:2 * DI])
            tap("ynt", ynt_ap[:, 0:NXT * TB])
            tap("x1", ar_x1[:, :])
            tap("dt", dtval[:, :])
            tap("lat", lat[:, :])
            tap("laT", laT[:, :])
            tap("elac", elac[:, :])
            tap("dtraw", dtraw[:, :])
        for blk in range(nb):
            emit_ln_uT(dx, blk)
            if dev_stop == "ln":
                continue
            emit_inproj_xbcdt(blk, "head")
            emit_dt_head(blk)
            emit_inproj_xbcdt(blk, "rest")
            if dev_stop == "inproj":
                continue
            emit_dt_tail(blk, with_wcols=False)
            emit_conv(blk)
            if dev_stop == "conv":
                continue
            emit_z_early(blk)
            st["ut_free"] = TE.n
            if dev_stop != "dt":
                for c in range(NCH):
                    emit_scan_chunk(blk, c, "main")
                if dev_stop != "scan":
                    emit_gated_ynt(blk)
                    if dev_stop != "gated":
                        emit_outproj_x1(blk)
                        if dev_stop != "opj":
                            emit_ffn_out(blk)
            st["stagefree_v"] = VE.n
            st["stagefree_a"] = AC.n
            st["stagefree_g"] = GP.n
            st["stagefree_t"] = TE.n

        if debug_tap is not None:
            names = (debug_tap,) if isinstance(debug_tap, str) else tuple(debug_tap)
            wait("scalar", VE, VE.n)
            wait("scalar", TE, TE.n)
            wait("scalar", AC, AC.n)
            wait("scalar", GP, GP.n)
            off = 0
            dbg_layout = {}
            for nm in names:
                src = taps[nm]
                pw = src.shape[1] if len(src.shape) == 2 else int(np.prod(src.shape[1:]))
                prt = src.shape[0]
                dbg_layout[nm] = (off, prt, pw)
                if sDBG.n:
                    wait("scalar", sDBG, sDBG.n)
                dma("scalar", ddbg[0:prt, off:off + pw], src, sDBG)
                off += pw
            assert off <= dbg_cols, f"debug taps too wide: {off}"
            build_nc.dbg_layout = dbg_layout

        wait("sync", sOUT, sOUT.n)
        wait("sync", sPO0, sPO0.n)
        wait("sync", sPO1, sPO1.n)
        wait("sync", sDBG, sDBG.n)

        with nc.Block() as block:
            block.sync(lambda e: [f(e) for f in streams["sync"]])
            block.tensor(lambda e: [f(e) for f in streams["tensor"]])
            block.vector(lambda e: [f(e) for f in streams["vector"]])
            block.scalar(lambda e: [f(e) for f in streams["scalar"]])
            block.gpsimd(lambda e: [f(e) for f in streams["gpsimd"]])

    return nc


def _host_prep(inputs):
    f32 = np.float32
    x = np.asarray(inputs["L0_attn_hidden_states"], dtype=f32)
    WT = np.ascontiguousarray(np.asarray(inputs["L0_attn_in_proj_w"], f32).T)
    WoT = np.ascontiguousarray(np.asarray(inputs["L0_attn_out_proj_w"], f32).T)
    wg = np.asarray(inputs["L0_ffn_w_gating"], f32)
    wtb = []
    for et in range(66):
        ec = 128 if et < 65 else 64
        wtb.append(WT[:, et * 128: et * 128 + ec]
                   .reshape(16, 128, ec).transpose(1, 0, 2).reshape(128, 16 * ec))
    WTb = np.ascontiguousarray(np.concatenate(wtb, axis=1))
    wotb = []
    for dmb in range(4):
        for kg in range(8):
            wotb.append(WoT[kg * 512:(kg + 1) * 512, dmb * 512:(dmb + 1) * 512]
                        .reshape(4, 128, 512).transpose(1, 0, 2).reshape(128, 2048))
    WoTb = np.ascontiguousarray(np.concatenate(wotb, axis=1))
    shared = {
        "WTb": WTb, "WoTb": WoTb,
        "Wg0": np.ascontiguousarray(wg[0]), "Wg1": np.ascontiguousarray(wg[1]),
        "Wl": np.ascontiguousarray(np.asarray(inputs["L0_ffn_w_linear"], f32)),
        "convw": np.ascontiguousarray(
            np.asarray(inputs["L0_attn_conv_w"], f32).reshape(NCT, 128, 4).transpose(1, 0, 2)),
        "convb": np.ascontiguousarray(
            np.asarray(inputs["L0_attn_conv_b"], f32).reshape(NCT, 128).T),
        "rms": np.ascontiguousarray(
            np.asarray(inputs["L0_attn_rms_weight"], f32).reshape(NKT, 128).T),
        "normw": np.ascontiguousarray(
            np.asarray(inputs["L0_attn_norm_w"], f32).reshape(NXT, 128).T),
        "dtb": np.asarray(inputs["L0_attn_dt_bias"], f32).reshape(64, 1).copy(),
        "alog": np.asarray(inputs["L0_attn_A_log"], f32).reshape(64, 1).copy(),
        "Drep": np.ascontiguousarray(np.broadcast_to(
            np.asarray(inputs["L0_attn_D"], f32).reshape(1, 64), (128, 64))),
        "ident32": np.eye(128, dtype=f32),
        "identbf": np.eye(128, dtype=f32).astype(ml_dtypes.bfloat16),
        "onesc": np.ones((128, 128), dtype=f32),
        "sunegt4": np.ascontiguousarray(np.tile(
            np.where(np.arange(128)[None, :] < np.arange(128)[:, None],
                     f32(-30000.0), f32(0.0)).astype(f32), (1, 4))),
        "epsone": np.ascontiguousarray(
            np.broadcast_to(np.array([EPS, 1.0], dtype=f32), (128, 2))),
    }
    per_core = []
    zeros = np.zeros((2048, D), dtype=f32)
    for c in range(8):
        b, h = c // 2, c % 2
        m = dict(shared)
        m["x_own"] = np.ascontiguousarray(x[b, h * 2048:(h + 1) * 2048, :])
        m["x_prev"] = np.ascontiguousarray(x[b, 0:2048, :]) if h == 1 else zeros
        m["flag"] = np.full((64, 1), f32(h), dtype=f32)
        per_core.append(m)
    return per_core


def kernel(**inputs):
    per_core = _host_prep(inputs)
    key = (DEV_NB, DEV_SKIP_RECOMPUTE, DEBUG_TAP, DEV_STOP)
    if key not in _BUILD_CACHE:
        _BUILD_CACHE[key] = build_nc(DEV_NB, DEV_SKIP_RECOMPUTE, DEBUG_TAP, DEV_STOP)
    nc = _BUILD_CACHE[key]
    res = run_bass_kernel_spmd(nc, per_core, core_ids=list(range(8)))
    out = np.empty((4, 4096, D), dtype=np.float32)
    for c in range(8):
        b, h = c // 2, c % 2
        out[b, h * 2048:(h + 1) * 2048, :] = np.asarray(res.results[c]["out"])
    if DEBUG_TAP is not None:
        kernel.dbg = [np.asarray(r["dbg"]) for r in res.results]
    return out
